# version 06
# baseline (speedup 1.0000x reference)
import math
import os
import numpy as np
import concourse.bass as bass
import concourse.mybir as mybir
from concourse.bass_utils import run_bass_kernel_spmd

os.environ.setdefault("GAUGE_NTFF_IGNORE_DMA", "1")

F32 = mybir.dt.float32
BF16 = mybir.dt.bfloat16
I32 = mybir.dt.int32
AX = mybir.AxisListType
ALU = mybir.AluOpType
AF = mybir.ActivationFunctionType

D = 2048
T = 2048
NE = 8
FE = 1408
FS = 2816
FH = 8192
NH0, KV0, HD0 = 32, 8, 64
NH1, KV1, HD1 = 16, 4, 128
SM0 = 1.0 / math.sqrt(HD0)
SM1 = 1.0 / math.sqrt(HD1)
NEGBIG = -1e30

N_STAGES = None
DUMP = []


class Sem:
    def __init__(self, handle):
        self.h = handle
        self.n = 0

    def inc(self, ins, by=1):
        ins.then_inc(self.h, by)
        self.n += by
        return self.n

    def now(self):
        return (self, self.n)


_WCACHE = {}


def wt(eng, *waits):
    key0 = eng.engine

    def one(w):
        k = (key0, id(w[0]))
        if _WCACHE.get(k, -1) >= w[1]:
            return
        _WCACHE[k] = w[1]
        eng.wait_ge(w[0].h, w[1])

    for w in waits:
        if w is None:
            continue
        if isinstance(w, list):
            for x in w:
                if x is not None:
                    one(x)
        else:
            one(w)


class DRing:
    def __init__(self, ctx, ap, nslots, width, loads=None, stores=None):
        self.ap = ap
        self.ns = nslots
        self.w = width
        self.ls = [ctx.dsem(f"{loads}{i}") for i in range(nslots)] if loads else None
        self.ss = [ctx.dsem(f"{stores}{i}") for i in range(nslots)] if stores else None
        self.reuse = [[] for _ in range(nslots)]
        self.i = 0

    def take(self):
        s = self.i % self.ns
        self.i += 1
        return RSlot(self, s)

    def release(self, s, *semvals):
        self.reuse[s] = [sv for sv in semvals if sv is not None]


class RSlot:
    def __init__(self, ring, s):
        self.ring = ring
        self.s = s
        self.ap = ring.ap[:, s * ring.w:(s + 1) * ring.w]
        self.lsem = ring.ls[s] if ring.ls else None
        self.ssem = ring.ss[s] if ring.ss else None
        self.reuse = list(ring.reuse[s])

    def release(self, *semvals):
        self.ring.release(self.s, *semvals)


class Ctx:
    def __init__(self, nc, blk):
        self.nc = nc
        self.blk = blk
        self.psum = []
        self.bankw = [None] * 8
        self.dsems = {}
        self.dirty = {}
        self.barwaits = []

    def dsem(self, name):
        if name not in self.dsems:
            self.dsems[name] = Sem(self.semalloc(name))
        return self.dsems[name]

    def load(self, q, out, in_, sem):
        sem.inc(q.dma_start(out=out, in_=in_), 16)

    def store(self, q, out, in_, sem):
        sem.inc(q.dma_start(out=out, in_=in_), 16)
        self.dirty[sem] = sem.n

    def barrier(self):
        self.barwaits = [(s, n) for s, n in self.dirty.items()]
        self.dirty = {}
        self.bankw = [None] * 8

    def begin(self):
        b = list(self.barwaits)
        blk = self.blk

        @blk.sync
        def _(q):
            wt(q, b)

        @blk.tensor
        def _(te):
            wt(te, b)

        @blk.vector
        def _(v):
            wt(v, b)

        @blk.scalar
        def _(s):
            wt(s, b)

    def bankwait(self, eng, bank):
        w = self.bankw[bank]
        if w is not None:
            wt(eng, w)


def emit_round(v, y, scr, scr2, i32t):
    W = y.shape[1]
    i32v = i32t[:, 0:W]
    v.drain()
    v.tensor_copy(i32v, y)
    v.drain()
    v.tensor_copy(scr, i32v)
    v.drain()
    v.tensor_sub(y, y, scr)
    v.drain()
    v.tensor_single_scalar(scr2, y, 0.5, ALU.is_ge)
    v.drain()
    v.tensor_add(scr, scr, scr2)
    v.tensor_single_scalar(y, y, -0.5, ALU.is_le)
    v.drain()
    v.tensor_sub(y, scr, y)
    v.drain()


def gemm(ctx, out, wT, rhs, K, M, N, *,
         dt=F32, scale=None, scale_mtiles=None, mul_hbm=None, act=None,
         addf=None, stats=None, stats_hbm=None, t_block=1024):
    blk = ctx.blk
    esz = 4 if dt == F32 else 2
    KC = K // 128
    MT = M // 128
    TB = t_block
    NB = N // TB
    NT = TB // 512
    wcap = 4096 * 4 // esz
    GM = 2 if KC * 256 <= wcap else 1
    NG = (MT + GM - 1) // GM
    assert KC * TB * esz <= 65536, "rhs slab too large"
    assert KC * GM * 128 <= wcap, "weight slab too large"

    slab = ctx.rA if dt == F32 else ctx.rA.bitcast(BF16)
    wbufs = [ctx.rW0, ctx.rW1] if dt == F32 else \
            [ctx.rW0.bitcast(BF16), ctx.rW1.bitcast(BF16)]
    lda = ctx.dsem("lda")
    lw = [ctx.dsem("lw0"), ctx.dsem("lw1")]
    er = DRing(ctx, ctx.rE, 8, 512, stores="se")
    fr = DRing(ctx, ctx.rF, 8, 512, loads="lf") if addf is not None else None
    mulsb = ctx.rS[:, 0:N] if mul_hbm is not None else None
    if mul_hbm is not None:
        @blk.sync
        def _(q):
            ctx.load(q, mulsb, mul_hbm, lda)
    if stats is not None:
        @blk.vector
        def _(v):
            ctx.ev.inc(v.memset(stats, 0.0))

    for tb in range(NB):
        mm_before = ctx.mm.n

        @blk.sync
        def _(q, tb=tb):
            if tb > 0:
                wt(q, (ctx.mm, mm_before))
            for c in range(KC):
                ctx.load(q, slab[:, c * TB:(c + 1) * TB],
                         rhs[c * 128:(c + 1) * 128, tb * TB:(tb + 1) * TB], lda)
        rhs_ld = lda.now()

        wgrp_after = {}
        for g in range(NG):
            mts = list(range(g * GM, min(g * GM + GM, MT)))
            gw = len(mts) * 128
            wb = wbufs[g % 2]
            ws = lw[g % 2]

            grouped = len(wT.shape) == 3

            @blk.sync
            def _(q, g=g, mts=mts, gw=gw, wb=wb, ws=ws, tb=tb, grouped=grouped):
                if g >= 2:
                    wt(q, (ctx.mm, wgrp_after[g - 2]))
                elif tb > 0:
                    wt(q, (ctx.mm, mm_before))
                if grouped:
                    ctx.load(q, wb[:, 0:KC * gw], wT[g], ws)
                else:
                    for c in range(KC):
                        ctx.load(q, wb[:, c * gw:(c + 1) * gw],
                                 wT[c * 128:(c + 1) * 128,
                                    mts[0] * 128:mts[0] * 128 + gw], ws)
            w_ld = ws.now()

            for mi, mt in enumerate(mts):
                for t in range(NT):
                    tg = tb * NT + t
                    bank = (mt * NT + t) % 4
                    pb = ctx.psum[bank]

                    @blk.tensor
                    def _(te, mi=mi, t=t, bank=bank, pb=pb, gw=gw, wb=wb):
                        wt(te, rhs_ld, w_ld)
                        ctx.bankwait(te, bank)
                        ins = None
                        for c in range(KC):
                            ins = te.matmul(
                                pb,
                                wb[:, c * gw + mi * 128: c * gw + (mi + 1) * 128],
                                slab[:, c * TB + t * 512: c * TB + t * 512 + 512],
                                start=(c == 0), stop=(c == KC - 1))
                        ctx.mm.inc(ins)
                    mm_v = ctx.mm.now()

                    out_ap = out[mt * 128:(mt + 1) * 128, tg * 512:(tg + 1) * 512]
                    esl = er.take()
                    et = esl.ap
                    sc_imm = scale
                    if scale_mtiles is not None and sc_imm is None:
                        sc_imm = scale_mtiles[1] if mt < scale_mtiles[0] else None

                    fsl = f_ld = None
                    if addf is not None:
                        fsl = fr.take()

                        @blk.sync
                        def _(q, fsl=fsl, mt=mt, tg=tg):
                            wt(q, fsl.reuse)
                            ctx.load(q, fsl.ap, addf[
                                mt * 128:(mt + 1) * 128,
                                tg * 512:(tg + 1) * 512], fsl.lsem)
                        f_ld = fsl.lsem.now()

                    prod = None
                    if mul_hbm is not None:
                        @blk.vector
                        def _(v, et=et, pb=pb, tg=tg, esl=esl):
                            wt(v, mm_v, esl.reuse, rhs_ld)
                            ctx.ev.inc(v.tensor_mul(
                                et, pb, mulsb[:, tg * 512:(tg + 1) * 512]))
                        ctx.bankw[bank] = ctx.ev.now()
                        prod = ctx.ev.now()

                    if act is not None or prod is None:
                        @blk.scalar
                        def _(s, et=et, pb=pb, prod=prod, esl=esl, sc_imm=sc_imm):
                            if prod is not None:
                                wt(s, prod)
                                src = et
                            else:
                                wt(s, mm_v, esl.reuse)
                                src = pb
                            if act is not None:
                                ctx.sc.inc(s.activation(
                                    et, src, act,
                                    scale=1.0 if sc_imm is None else sc_imm))
                            elif sc_imm is not None:
                                ctx.sc.inc(s.mul(et, src, sc_imm))
                            else:
                                ctx.sc.inc(s.copy(et, src))
                        if prod is None:
                            ctx.bankw[bank] = ctx.sc.now()
                        prod = ctx.sc.now()
                    elif sc_imm is not None:
                        @blk.vector
                        def _(v, et=et, sc_imm=sc_imm):
                            v.drain()
                            ctx.ev.inc(v.tensor_scalar_mul(et, et, sc_imm))
                        prod = ctx.ev.now()

                    stat_sem = None
                    if stats is not None:
                        @blk.vector
                        def _(v, et=et, tg=tg, prod=prod):
                            wt(v, prod)
                            stt = stats[:, tg * 512:(tg + 1) * 512]
                            v.drain()
                            v.tensor_tensor(stt, stt, et, ALU.max)
                            v.drain()
                            ctx.ev.inc(v.scalar_tensor_tensor(
                                stt, et, -1.0, stt, ALU.mult, ALU.max))
                        stat_sem = ctx.ev.now()

                    if addf is not None:
                        @blk.vector
                        def _(v, et=et, fsl=fsl, prod=prod, f_ld=f_ld):
                            if prod[0] is not ctx.ev:
                                wt(v, prod)
                            else:
                                v.drain()
                            wt(v, f_ld)
                            ctx.ev.inc(v.tensor_add(et, et, fsl.ap))
                        prod = ctx.ev.now()

                    @blk.scalar
                    def _(s, et=et, out_ap=out_ap, prod=prod, esl=esl):
                        wt(s, prod)
                        ctx.store(s, out_ap, et, esl.ssem)
                    stv = esl.ssem.now()
                    esl.release(stv, stat_sem)
                    if addf is not None:
                        fsl.release(stv)
            wgrp_after[g] = ctx.mm.n

    if stats is not None and stats_hbm is not None:
        ev_fin = ctx.ev.now()
        so = ctx.dsem("so0")

        @blk.scalar
        def _(s):
            wt(s, ev_fin)
            ctx.store(s, stats_hbm, stats, so)
    ctx.barrier()


def bcast_cols(ctx, pairs, nj, extra_wait=None):
    blk = ctx.blk
    ev0 = ctx.ev.n
    for i, (src, dst) in enumerate(pairs):
        for j in range(nj):
            k = (i * nj + j) % 2
            bankT = 6 + k
            bankB = 4 + k
            tr = ctx.trow[0:1, k * 128:(k + 1) * 128]

            @blk.tensor
            def _(te, src=src, j=j, bankT=bankT):
                wt(te, (ctx.ev, ev0))
                ctx.bankwait(te, bankT)
                ctx.mm.inc(te.matmul(ctx.psum[bankT][0:1, 0:128],
                                     src[:, j:j + 1], ctx.idn,
                                     start=True, stop=True))
            mv1 = ctx.mm.now()

            @blk.vector
            def _(v, tr=tr, bankT=bankT, mv1=mv1):
                wt(v, mv1)
                ctx.ev.inc(v.tensor_copy(tr, ctx.psum[bankT][0:1, 0:128]))
            ctx.bankw[bankT] = ctx.ev.now()
            trv = ctx.ev.now()

            @blk.tensor
            def _(te, tr=tr, bankB=bankB, trv=trv):
                wt(te, trv)
                ctx.bankwait(te, bankB)
                ctx.mm.inc(te.matmul(ctx.psum[bankB][:, 0:128], ctx.ones_row,
                                     tr, start=True, stop=True))
            mv2 = ctx.mm.now()

            @blk.vector
            def _(v, j=j, dst=dst, bankB=bankB, mv2=mv2):
                wt(v, mv2, extra_wait)
                ctx.ev.inc(v.tensor_copy(dst[:, j * 128:(j + 1) * 128],
                                         ctx.psum[bankB][:, 0:128]))
            ctx.bankw[bankB] = ctx.ev.now()


def norm_fm(ctx, out, in_, *, kind="ln", eps=1e-5, gamma=None):
    blk = ctx.blk
    TB = 1024
    slab = ctx.rA
    lda = ctx.dsem("lda")
    st = ctx.stats
    S16, Q16 = st[:, 0:16], st[:, 16:32]
    NM16, RS16 = st[:, 32:48], st[:, 48:64]
    VP16, SD16 = st[:, 64:80], st[:, 80:96]
    er = DRing(ctx, ctx.rE, 4, 1024, stores="se")

    for tb in range(2):
        prev_stores = [(s, n) for s, n in ctx.dirty.items()]

        @blk.sync
        def _(q, tb=tb):
            if tb > 0:
                wt(q, prev_stores)
            for c in range(16):
                ctx.load(q, slab[:, c * TB:(c + 1) * TB],
                         in_[c * 128:(c + 1) * 128, tb * TB:(tb + 1) * TB], lda)
        ld_v = lda.now()

        sq_free = [None, None]
        for j in range(8):
            bs, bq = j % 2, 2 + (j % 2)
            for c in range(16):
                sq = ctx.sqst[:, (c % 2) * 128:(c % 2) * 128 + 128]

                @blk.vector
                def _(v, c=c, j=j, sq=sq):
                    wt(v, ld_v, sq_free[c % 2])
                    xs = slab[:, c * TB + j * 128: c * TB + j * 128 + 128]
                    ctx.ev.inc(v.tensor_mul(sq, xs, xs))
                sc_v = ctx.ev.now()

                @blk.tensor
                def _(te, c=c, j=j, bs=bs, bq=bq, sq=sq):
                    wt(te, sc_v)
                    if c == 0:
                        ctx.bankwait(te, bs)
                        ctx.bankwait(te, bq)
                    if kind == "ln":
                        te.matmul(ctx.psum[bs][:, 0:1],
                                  slab[:, c * TB + j * 128: c * TB + j * 128 + 128],
                                  ctx.ones_col, start=(c == 0), stop=(c == 15))
                    ctx.mm.inc(te.matmul(ctx.psum[bq][:, 0:1], sq, ctx.ones_col,
                                         start=(c == 0), stop=(c == 15)))
                sq_free[c % 2] = ctx.mm.now()
            mm_v = ctx.mm.now()

            @blk.vector
            def _(v, j=j, bs=bs, bq=bq):
                wt(v, mm_v)
                if kind == "ln":
                    v.tensor_copy(S16[:, j:j + 1], ctx.psum[bs][:, 0:1])
                ctx.ev.inc(v.tensor_copy(Q16[:, j:j + 1], ctx.psum[bq][:, 0:1]))
            ctx.bankw[bs] = ctx.ev.now()
            ctx.bankw[bq] = ctx.ev.now()

        @blk.vector
        def _(v):
            wt(v, ctx.ev.now())
            DR = v.drain
            S8, Q8 = S16[:, 0:8], Q16[:, 0:8]
            NM8, VP8, SD8 = NM16[:, 0:8], VP16[:, 0:8], SD16[:, 0:8]
            if kind == "ln":
                v.tensor_scalar_mul(NM8, S8, -1.0 / 2048.0)
                v.tensor_scalar_mul(VP8, Q8, 1.0 / 2048.0)
                DR()
                v.tensor_mul(SD8, NM8, NM8)
                DR()
                v.tensor_sub(VP8, VP8, SD8)
                DR()
                ctx.ev.inc(v.tensor_scalar_add(VP8, VP8, eps))
            else:
                ctx.ev.inc(v.tensor_scalar(VP8, Q8, 1.0 / 2048.0, eps,
                                           ALU.mult, ALU.add))
        evv = ctx.ev.now()

        @blk.scalar
        def _(s):
            wt(s, evv)
            ctx.sc.inc(s.activation(SD16[:, 0:8], VP16[:, 0:8], AF.Sqrt))
        scv = ctx.sc.now()

        @blk.vector
        def _(v):
            wt(v, scv)
            v.reciprocal(RS16[:, 0:8], SD16[:, 0:8])
            v.drain()
            ctx.ev.inc(v.tensor_copy(RS16[:, 0:8], RS16[:, 0:8]))

        NMb = ctx.rS[:, 0:1024]
        RSb = ctx.rS[:, 1024:2048]
        if kind == "ln":
            bcast_cols(ctx, [(NM16[:, 0:8], NMb), (RS16[:, 0:8], RSb)], 8)
        else:
            bcast_cols(ctx, [(RS16[:, 0:8], RSb)], 8)
        bc_v = ctx.ev.now()

        for c in range(16):
            esl = er.take()
            et = esl.ap

            @blk.vector
            def _(v, c=c, et=et, esl=esl):
                wt(v, bc_v, esl.reuse)
                xc = slab[:, c * TB:(c + 1) * TB]
                if kind == "ln":
                    v.tensor_add(et, xc, NMb)
                    v.drain()
                    ctx.ev.inc(v.tensor_mul(et, et, RSb))
                else:
                    wt(v, ctx.gamma_ld)
                    v.tensor_mul(et, xc, RSb)
                    v.drain()
                    ctx.ev.inc(v.tensor_single_scalar(
                        et, et, gamma[:, c:c + 1], ALU.mult))
            ev_v = ctx.ev.now()

            @blk.scalar
            def _(s, c=c, et=et, ev_v=ev_v, tb=tb, esl=esl):
                wt(s, ev_v)
                ctx.store(s, out[c * 128:(c + 1) * 128, tb * TB:(tb + 1) * TB],
                          et, esl.ssem)
            esl.release(esl.ssem.now())
    ctx.barrier()


def resadd(ctx, out, a, b, half=0.5):
    blk = ctx.blk
    ar = DRing(ctx, ctx.rE, 2, 2048, loads="la", stores="se")
    br = DRing(ctx, ctx.rF, 2, 2048, loads="lb")
    for c in range(16):
        asl = ar.take()
        bsl = br.take()

        @blk.sync
        def _(q, c=c, asl=asl, bsl=bsl):
            wt(q, asl.reuse, bsl.reuse)
            ctx.load(q, asl.ap, a[c * 128:(c + 1) * 128, :], asl.lsem)
            ctx.load(q, bsl.ap, b[c * 128:(c + 1) * 128, :], bsl.lsem)
        la, lb = asl.lsem.now(), bsl.lsem.now()

        @blk.vector
        def _(v, asl=asl, bsl=bsl):
            wt(v, la, lb)
            ctx.ev.inc(v.scalar_tensor_tensor(asl.ap, bsl.ap, half, asl.ap,
                                              ALU.mult, ALU.add))
        evv = ctx.ev.now()

        @blk.scalar
        def _(s, c=c, asl=asl, evv=evv):
            wt(s, evv)
            ctx.store(s, out[c * 128:(c + 1) * 128, :], asl.ap, asl.ssem)
        stv = asl.ssem.now()
        asl.release(stv)
        bsl.release(stv)
    ctx.barrier()


def rope(ctx, buf, nchunks, cos_hbm, sin_hbm, pswap_hbm, row0=0):
    blk = ctx.blk
    C = ctx.rC[:, 0:2048]
    S = ctx.rC[:, 2048:4096]
    lc = ctx.dsem("lc")

    @blk.sync
    def _(q):
        ctx.load(q, C, cos_hbm, lc)
        ctx.load(q, S, sin_hbm, lc)
        ctx.load(q, ctx.pswap, pswap_hbm, lc)
    cld = lc.now()

    xr = DRing(ctx, ctx.rE, 2, 2048, loads="la")
    orr = DRing(ctx, ctx.rF, 2, 2048, stores="se")
    for c in range(nchunks):
        xsl = xr.take()
        osl = orr.take()
        rows = buf[row0 + c * 128: row0 + (c + 1) * 128, :]

        @blk.sync
        def _(q, rows=rows, xsl=xsl, osl=osl):
            wt(q, cld, xsl.reuse, osl.reuse)
            ctx.load(q, xsl.ap, rows, xsl.lsem)
        ldv = xsl.lsem.now()

        for t in range(4):
            bank = t % 2
            sl = slice(t * 512, (t + 1) * 512)

            @blk.tensor
            def _(te, xsl=xsl, sl=sl, bank=bank, ldv=ldv):
                wt(te, ldv, cld)
                ctx.bankwait(te, bank)
                ctx.mm.inc(te.matmul(ctx.psum[bank], ctx.pswap, xsl.ap[:, sl],
                                     start=True, stop=True))
            mmv = ctx.mm.now()

            @blk.vector
            def _(v, xsl=xsl, osl=osl, sl=sl, bank=bank, mmv=mmv):
                wt(v, mmv)
                v.tensor_mul(osl.ap[:, sl], xsl.ap[:, sl], C[:, sl])
                v.drain()
                v.tensor_mul(xsl.ap[:, sl], ctx.psum[bank], S[:, sl])
                v.drain()
                ctx.ev.inc(v.tensor_add(osl.ap[:, sl], osl.ap[:, sl],
                                        xsl.ap[:, sl]))
            ctx.bankw[bank] = ctx.ev.now()
        evv = ctx.ev.now()

        @blk.scalar
        def _(s, rows=rows, osl=osl, evv=evv):
            wt(s, evv)
            ctx.store(s, rows, osl.ap, osl.ssem)
        stv = osl.ssem.now()
        xsl.release(stv)
        osl.release(stv)
    ctx.barrier()


def transpose_fm(ctx, out, in_, M, N):
    blk = ctx.blk
    xr = DRing(ctx, ctx.rE, 2, 2048, loads="la")
    orr = DRing(ctx, ctx.rF, 8, 512, stores="se")
    for c in range(M // 128):
        xsl = xr.take()

        @blk.sync
        def _(q, c=c, xsl=xsl):
            wt(q, xsl.reuse)
            ctx.load(q, xsl.ap[:, 0:N], in_[c * 128:(c + 1) * 128, :], xsl.lsem)
        ldv = xsl.lsem.now()

        last = None
        for b in range(N // 128):
            bank = b % 2
            osl = orr.take()

            @blk.tensor
            def _(te, b=b, xsl=xsl, bank=bank, ldv=ldv):
                wt(te, ldv)
                ctx.bankwait(te, bank)
                ctx.mm.inc(te.transpose(ctx.psum[bank][:, 0:128],
                                        xsl.ap[:, b * 128:(b + 1) * 128],
                                        ctx.idn))
            mmv = ctx.mm.now()

            @blk.vector
            def _(v, osl=osl, bank=bank, mmv=mmv):
                wt(v, mmv, osl.reuse)
                ctx.ev.inc(v.tensor_copy(osl.ap[:, 0:128],
                                         ctx.psum[bank][:, 0:128]))
            ctx.bankw[bank] = ctx.ev.now()
            evv = ctx.ev.now()

            @blk.scalar
            def _(s, b=b, c=c, osl=osl, evv=evv):
                wt(s, evv)
                ctx.store(s, out[b * 128:(b + 1) * 128, c * 128:(c + 1) * 128],
                          osl.ap[:, 0:128], osl.ssem)
            osl.release(osl.ssem.now())
            last = osl.ssem.now()
        xsl.release(last)
    ctx.barrier()


def router(ctx, WB, gwT, LN):
    blk = ctx.blk
    TB = 1024
    slab = ctx.rA
    gw = ctx.gw_sb
    lda = ctx.dsem("lda")
    lc = ctx.dsem("lc")
    so = ctx.dsem("so1")

    @blk.sync
    def _(q):
        for c in range(16):
            ctx.load(q, gw[:, c * 8:(c + 1) * 8], gwT[c * 128:(c + 1) * 128, :], lc)
    gw_ld = lc.now()

    g = ctx.gsmall
    LT = g[:, 0:64]
    ET = g[:, 64:128]
    M8 = g[:, 128:192]
    W1 = g[:, 192:256]
    M2 = g[:, 256:320]
    GS = g[:, 320:352]
    GS2 = g[:, 352:384]
    E1 = g[:, 384:416]
    E2 = g[:, 416:448]
    GSEL = g[:, 448:480]
    NM = g[:, 480:496]
    SU = g[:, 496:512]
    RV = g[:, 512:528]
    G1 = g[:, 528:544]
    T1 = g[:, 544:560]
    LG = ctx.rC[:, 0:1024]

    for tb in range(2):
        prev_stores = [(s, n) for s, n in ctx.dirty.items()]

        @blk.sync
        def _(q, tb=tb):
            wt(q, gw_ld)
            if tb > 0:
                wt(q, prev_stores)
            for c in range(16):
                ctx.load(q, slab[:, c * TB:(c + 1) * TB],
                         LN[c * 128:(c + 1) * 128, tb * TB:(tb + 1) * TB], lda)
        ld_v = lda.now()

        for t in range(2):
            bank = t % 2

            @blk.tensor
            def _(te, t=t, bank=bank):
                wt(te, ld_v, gw_ld)
                ctx.bankwait(te, bank)
                ins = None
                for c in range(16):
                    ins = te.matmul(ctx.psum[bank][0:8, :],
                                    gw[:, c * 8:(c + 1) * 8],
                                    slab[:, c * TB + t * 512: c * TB + t * 512 + 512],
                                    start=(c == 0), stop=(c == 15))
                ctx.mm.inc(ins)
            mmv = ctx.mm.now()

            @blk.vector
            def _(v, t=t, bank=bank, mmv=mmv):
                wt(v, mmv)
                ctx.ev.inc(v.tensor_copy(LG[0:8, t * 512:(t + 1) * 512],
                                         ctx.psum[bank][0:8, :]))
            ctx.bankw[bank] = ctx.ev.now()
        lg_v = ctx.ev.now()

        for j in range(8):
            LTj = LT[:, j * 8:(j + 1) * 8]
            ETj = ET[:, j * 8:(j + 1) * 8]
            M8j = M8[:, j * 8:(j + 1) * 8]
            W1j = W1[:, j * 8:(j + 1) * 8]
            M2j = M2[:, j * 8:(j + 1) * 8]
            GSj = GS[:, j * 4:(j + 1) * 4]
            GS2j = GS2[:, j * 4:(j + 1) * 4]
            E1j = E1[:, j * 4:(j + 1) * 4]
            E2j = E2[:, j * 4:(j + 1) * 4]
            GSELj = GSEL[:, j * 4:(j + 1) * 4]
            NMj = NM[:, j:j + 1]
            SUj = SU[:, j:j + 1]
            RVj = RV[:, j:j + 1]
            G1j = G1[:, j:j + 1]
            T1j = T1[:, j:j + 1]
            bank = 2 + j % 2

            @blk.tensor
            def _(te, j=j, bank=bank, lg_v=lg_v):
                wt(te, lg_v)
                ctx.bankwait(te, bank)
                ctx.mm.inc(te.transpose(ctx.psum[bank][:, 0:8],
                                        LG[0:8, j * 128:(j + 1) * 128],
                                        ctx.idn[0:8, 0:8]))
            mmv = ctx.mm.now()

            @blk.vector
            def _(v, LTj=LTj, NMj=NMj, bank=bank, mmv=mmv):
                wt(v, mmv)
                v.tensor_copy(LTj, ctx.psum[bank][:, 0:8])
                v.drain()
                v.tensor_reduce(NMj, LTj, AX.X, ALU.max, negate=True)
                ctx.ev.inc(v.drain())
            ctx.bankw[bank] = ctx.ev.now()
            nm_v = ctx.ev.now()

            @blk.scalar
            def _(s, LTj=LTj, ETj=ETj, NMj=NMj, SUj=SUj, nm_v=nm_v):
                wt(s, nm_v)
                ctx.sc.inc(s.activation(ETj, LTj, AF.Exp, bias=NMj,
                                        accum_out=SUj))
            ex_v = ctx.sc.now()

            @blk.vector
            def _(v, j=j, ETj=ETj, M8j=M8j, W1j=W1j, M2j=M2j, GSj=GSj,
                  GS2j=GS2j, E1j=E1j, E2j=E2j, GSELj=GSELj, SUj=SUj,
                  RVj=RVj, G1j=G1j, T1j=T1j, ex_v=ex_v):
                DR = v.drain
                wt(v, ex_v)
                v.reciprocal(RVj, SUj)
                DR()
                v.tensor_single_scalar(ETj, ETj, RVj, ALU.mult)
                DR()
                v.tensor_tensor(GSj, ETj[:, 0:8:2], ETj[:, 1:8:2], ALU.max)
                DR()
                v.tensor_reduce(G1j, GSj, AX.X, ALU.max)
                DR()
                v.tensor_single_scalar(E1j, GSj, G1j, ALU.is_ge)
                DR()
                v.tensor_mul(E1j, E1j, GSj)
                DR()
                v.tensor_sub(GS2j, GSj, E1j)
                DR()
                v.tensor_reduce(G1j, GS2j, AX.X, ALU.max)
                DR()
                v.tensor_single_scalar(E2j, GS2j, G1j, ALU.is_ge)
                v.tensor_single_scalar(GSELj, E1j, 0.0, ALU.is_gt)
                DR()
                v.tensor_add(GSELj, GSELj, E2j)
                DR()
                v.tensor_mul(M8j[:, 0:8:2], ETj[:, 0:8:2], GSELj)
                v.tensor_mul(M8j[:, 1:8:2], ETj[:, 1:8:2], GSELj)
                DR()
                v.tensor_reduce(T1j, M8j, AX.X, ALU.max)
                DR()
                v.tensor_single_scalar(W1j, M8j, T1j, ALU.is_ge)
                DR()
                v.tensor_mul(W1j, W1j, M8j)
                DR()
                v.tensor_sub(M2j, M8j, W1j)
                DR()
                v.tensor_reduce(T1j, M2j, AX.X, ALU.max)
                DR()
                v.tensor_single_scalar(ETj, M2j, T1j, ALU.is_ge)
                DR()
                v.tensor_mul(M2j, M2j, ETj)
                DR()
                v.tensor_add(W1j, W1j, M2j)
                ctx.ev.inc(DR())

        prev_wb = None
        for n in range(NE):
            src = W1[:, n:64:8]
            dst = ctx.rF[:, 0:1024]
            bcast_cols(ctx, [(src, dst)], 8, extra_wait=prev_wb)
            bc_v = ctx.ev.now()

            @blk.scalar
            def _(s, n=n, tb=tb, dst=dst, bc_v=bc_v):
                wt(s, bc_v)
                ctx.store(s, WB[n, :, tb * TB:(tb + 1) * TB], dst, so)
            prev_wb = so.now()
    ctx.barrier()


def moe_hidden(ctx, HDN, GU, FF, wb_hbm=None):
    blk = ctx.blk
    lc = ctx.dsem("lc")
    wb_ld = None
    if wb_hbm is not None:
        @blk.sync
        def _(q):
            ctx.load(q, ctx.rS[:, 0:2048], wb_hbm, lc)
        wb_ld = lc.now()

    gr = DRing(ctx, ctx.rE, 2, 2048, loads="la")
    ur = DRing(ctx, ctx.rF, 2, 2048, loads="lb", stores="se")
    for fc in range(FF // 128):
        gsl = gr.take()
        usl = ur.take()

        @blk.sync
        def _(q, fc=fc, gsl=gsl, usl=usl):
            wt(q, gsl.reuse, usl.reuse)
            ctx.load(q, gsl.ap, GU[fc * 128:(fc + 1) * 128, :], gsl.lsem)
            ctx.load(q, usl.ap, GU[FF + fc * 128: FF + (fc + 1) * 128, :],
                     usl.lsem)
        lg, lu = gsl.lsem.now(), usl.lsem.now()

        @blk.scalar
        def _(s, gsl=gsl):
            wt(s, lg)
            ctx.sc.inc(s.activation(gsl.ap, gsl.ap, AF.Silu))
        scv = ctx.sc.now()

        @blk.vector
        def _(v, gsl=gsl, usl=usl):
            wt(v, scv, lu, wb_ld)
            if wb_hbm is not None:
                v.tensor_mul(usl.ap, gsl.ap, usl.ap)
                v.drain()
                ctx.ev.inc(v.tensor_mul(usl.ap, usl.ap, ctx.rS[:, 0:2048]))
            else:
                ctx.ev.inc(v.tensor_mul(usl.ap, gsl.ap, usl.ap))
        evv = ctx.ev.now()

        @blk.scalar
        def _(s, fc=fc, usl=usl, evv=evv):
            wt(s, evv)
            ctx.store(s, HDN[fc * 128:(fc + 1) * 128, :], usl.ap, usl.ssem)
        stv = usl.ssem.now()
        gsl.release(stv)
        usl.release(stv)
    ctx.barrier()


def attention(ctx, YA, QKV, VT, masks_hbm, *, nh, nkv, hd, causal):
    blk = ctx.blk
    A = ctx.rA
    kh = A[:, 0:2048]
    qh = A[:, 2048:4096]
    vt = A[:, 4096:6144]
    yb = A[:, 6144:8192]
    S4 = [A[:, 8192 + i * 2048: 8192 + (i + 1) * 2048] for i in range(4)]

    def PTc(c):
        r = ctx.rE if c < 8 else ctx.rF
        return r[:, (c % 8) * 512:(c % 8) * 512 + 512]

    g = ctx.gsmall
    nm = [g[:, i:i + 1] for i in range(4)]
    su = [g[:, 4 + i:5 + i] for i in range(4)]
    rv = [g[:, 8 + i:9 + i] for i in range(4)]
    MSK = ctx.rC[:, 0:2048]
    lc = ctx.dsem("lc")
    lk = ctx.dsem("lw0")
    lq = ctx.dsem("lw1")
    so = ctx.dsem("so1")

    if causal:
        @blk.sync
        def _(q):
            for a in range(4):
                ctx.load(q, MSK[:, a * 512:(a + 1) * 512], masks_hbm[a], lc)
    msk_ld = lc.now()
    rep = nh // nkv
    y_store = None

    for kv in range(nkv):
        @blk.sync
        def _(q, kv=kv):
            wt(q, ctx.mm.now())
            ctx.load(q, kh[0:hd, :],
                     QKV[2048 + kv * hd: 2048 + (kv + 1) * hd, :], lk)
            for c in range(16):
                ctx.load(q, vt[:, c * hd:(c + 1) * hd],
                         VT[c * 128:(c + 1) * 128, kv * hd:(kv + 1) * hd], lk)
        kv_ld = lk.now()

        for qi in range(rep):
            h = kv * rep + qi

            @blk.sync
            def _(q, h=h):
                wt(q, ctx.mm.now())
                ctx.load(q, qh[0:hd, :], QKV[h * hd:(h + 1) * hd, :], lq)
            q_ld = lq.now()

            for tqg in range(4):
                EXT = (tqg + 1) * 512 if causal else 2048
                NKC = EXT // 128
                exps = []
                for i in range(4):
                    tqt = tqg * 4 + i
                    S = S4[i]
                    last_ev = last_sc = None
                    for tkb in range(EXT // 512):
                        bank = (i * (EXT // 512) + tkb) % 2

                        @blk.tensor
                        def _(te, tqt=tqt, tkb=tkb, bank=bank):
                            wt(te, kv_ld, q_ld, msk_ld)
                            ctx.bankwait(te, bank)
                            ctx.mm.inc(te.matmul(
                                ctx.psum[bank],
                                qh[0:hd, tqt * 128:(tqt + 1) * 128],
                                kh[0:hd, tkb * 512:(tkb + 1) * 512],
                                start=True, stop=True))
                        mmv = ctx.mm.now()

                        diag = causal and (tkb == tqg)
                        if diag:
                            @blk.vector
                            def _(v, i=i, tqt=tqt, tkb=tkb, bank=bank, mmv=mmv, S=S):
                                wt(v, mmv)
                                ctx.ev.inc(v.tensor_add(
                                    S[:, tkb * 512:(tkb + 1) * 512],
                                    ctx.psum[bank],
                                    MSK[:, (tqt % 4) * 512:(tqt % 4 + 1) * 512]))
                            ctx.bankw[bank] = ctx.ev.now()
                            last_ev = ctx.ev.now()
                        else:
                            @blk.scalar
                            def _(s, tkb=tkb, bank=bank, mmv=mmv, S=S):
                                wt(s, mmv)
                                ctx.sc.inc(s.copy(
                                    S[:, tkb * 512:(tkb + 1) * 512],
                                    ctx.psum[bank]))
                            ctx.bankw[bank] = ctx.sc.now()
                            last_sc = ctx.sc.now()

                    @blk.vector
                    def _(v, i=i, S=S, EXT=EXT, last_ev=last_ev, last_sc=last_sc):
                        wt(v, last_ev, last_sc)
                        ctx.ev.inc(v.tensor_reduce(nm[i], S[:, 0:EXT], AX.X,
                                                   ALU.max, negate=True))
                    exps.append(ctx.ev.now())

                ex_vs = []
                for i in range(4):
                    @blk.scalar
                    def _(s, i=i, EXT=EXT):
                        wt(s, (ctx.ev, exps[i][1]))
                        S = S4[i]
                        ctx.sc.inc(s.activation(S[:, 0:EXT], S[:, 0:EXT],
                                                AF.Exp, bias=nm[i],
                                                accum_out=su[i]))
                    ex_vs.append(ctx.sc.now())

                @blk.vector
                def _(v, EXT=EXT, ex_vs=ex_vs):
                    wt(v, ex_vs[3])
                    for i in range(4):
                        v.reciprocal(rv[i], su[i])
                    v.drain()
                    for i in range(4):
                        v.tensor_single_scalar(S4[i][:, 0:EXT], S4[i][:, 0:EXT],
                                               rv[i], ALU.mult)
                    ctx.ev.inc(v.drain())
                exps = [ctx.ev.now()] * 4

                pt_evs = None
                for c in range(NKC):
                    bank = 2 + c % 2

                    @blk.tensor
                    def _(te, c=c, bank=bank, exps=exps):
                        wt(te, exps[3])
                        ctx.bankwait(te, bank)
                        ins = None
                        for i in range(4):
                            ins = te.transpose(
                                ctx.psum[bank][:, i * 128:(i + 1) * 128],
                                S4[i][:, c * 128:(c + 1) * 128], ctx.idn)
                        ctx.mm.inc(ins)
                    mv = ctx.mm.now()

                    @blk.vector
                    def _(v, c=c, bank=bank, mv=mv):
                        wt(v, mv)
                        ctx.ev.inc(v.tensor_copy(PTc(c), ctx.psum[bank]))
                    ctx.bankw[bank] = ctx.ev.now()
                    pt_evs = ctx.ev.now()

                @blk.tensor
                def _(te, EXT=EXT, NKC=NKC, pt_evs=pt_evs):
                    wt(te, pt_evs)
                    ctx.bankwait(te, 4)
                    ins = None
                    for c in range(NKC):
                        ins = te.matmul(ctx.psum[4][0:hd, :],
                                        vt[:, c * hd:(c + 1) * hd], PTc(c),
                                        start=(c == 0), stop=(c == NKC - 1))
                    ctx.mm.inc(ins)
                y_mm = ctx.mm.now()

                @blk.scalar
                def _(s, tqg=tqg, y_mm=y_mm, y_store=y_store):
                    wt(s, y_mm)
                    if tqg == 0:
                        wt(s, y_store)
                    ctx.sc.inc(s.copy(yb[0:hd, tqg * 512:(tqg + 1) * 512],
                                      ctx.psum[4][0:hd, :]))
                ctx.bankw[4] = ctx.sc.now()

            y_done = ctx.sc.now()

            @blk.scalar
            def _(s, h=h, y_done=y_done):
                wt(s, y_done)
                ctx.store(s, YA[h * hd:(h + 1) * hd, :], yb[0:hd, :], so)
            y_store = so.now()
    ctx.barrier()


def weight_quant(ctx, WQ, wT, K, M, INVW):
    blk = ctx.blk
    ntk = K // 128
    ntm = M // 2048
    NTT = ntk * ntm
    acc = ctx.rC[:, 0:NTT]
    xr = DRing(ctx, ctx.rE, 2, 2048, loads="la")
    so = ctx.dsem("so2")

    for i in range(NTT):
        ck, cm = i // ntm, i % ntm
        xsl = xr.take()

        @blk.sync
        def _(q, ck=ck, cm=cm, xsl=xsl):
            wt(q, xsl.reuse)
            ctx.load(q, xsl.ap, wT[ck * 128:(ck + 1) * 128,
                                   cm * 2048:(cm + 1) * 2048], xsl.lsem)
        ldv = xsl.lsem.now()

        @blk.vector
        def _(v, i=i, xsl=xsl, ldv=ldv):
            wt(v, ldv)
            ctx.ev.inc(v.tensor_reduce(acc[:, i:i + 1], xsl.ap, AX.X, ALU.add,
                                       apply_absolute_value=True))
        xsl.release(ctx.ev.now())

    SW = ctx.gsmall[:, 0:1]
    IW = ctx.gsmall[:, 1:2]
    t11 = ctx.gsmall[:, 2:3]

    @blk.vector
    def _(v):
        wt(v, ctx.ev.now())
        v.tensor_reduce(t11, acc[:, 0:NTT], AX.X, ALU.add)
        ctx.ev.inc(v.drain())
    rs_v = ctx.ev.now()

    @blk.tensor
    def _(te, rs_v=rs_v):
        wt(te, rs_v)
        ctx.bankwait(te, 6)
        ctx.mm.inc(te.transpose(ctx.psum[6][0:1, 0:128], t11, ctx.idn))
    tr_v = ctx.mm.now()

    @blk.vector
    def _(v, tr_v=tr_v):
        wt(v, tr_v)
        row = ctx.trow[0:1, 0:128]
        v.tensor_copy(row, ctx.psum[6][0:1, 0:128])
        v.drain()
        v.tensor_reduce(ctx.trow[0:1, 128:129], row, AX.X, ALU.add)
        v.drain()
        v.tensor_scalar(ctx.trow[0:1, 129:130], ctx.trow[0:1, 128:129],
                        1.0 / (K * M), 1e-5, ALU.mult, ALU.max)
        v.drain()
        v.reciprocal(ctx.trow[0:1, 130:131], ctx.trow[0:1, 129:130])
        ctx.ev.inc(v.drain())
    ctx.bankw[6] = ctx.ev.now()
    sw_v = ctx.ev.now()

    for src_off, dst in ((130, SW), (129, IW)):
        @blk.tensor
        def _(te, src_off=src_off, sw_v=sw_v):
            wt(te, sw_v)
            ctx.bankwait(te, 7)
            ctx.mm.inc(te.matmul(ctx.psum[7][:, 0:1], ctx.ones_row,
                                 ctx.trow[0:1, src_off:src_off + 1],
                                 start=True, stop=True))
        bv = ctx.mm.now()

        @blk.vector
        def _(v, dst=dst, bv=bv):
            wt(v, bv)
            ctx.ev.inc(v.tensor_copy(dst, ctx.psum[7][:, 0:1]))
        ctx.bankw[7] = ctx.ev.now()
    swb_v = ctx.ev.now()

    @blk.scalar
    def _(s, swb_v=swb_v):
        wt(s, swb_v)
        ctx.store(s, INVW, IW, so)

    br = DRing(ctx, ctx.rB.bitcast(BF16), 2, 2048, stores="se")
    scr = ctx.rF[:, 0:2048]
    scr2 = ctx.rF[:, 2048:4096]
    for i in range(NTT):
        ck, cm = i // ntm, i % ntm
        xsl = xr.take()
        bsl = br.take()

        @blk.sync
        def _(q, ck=ck, cm=cm, xsl=xsl):
            wt(q, xsl.reuse)
            ctx.load(q, xsl.ap, wT[ck * 128:(ck + 1) * 128,
                                   cm * 2048:(cm + 1) * 2048], xsl.lsem)
        ldv = xsl.lsem.now()

        @blk.vector
        def _(v, xsl=xsl, bsl=bsl, ldv=ldv, swb_v=swb_v):
            wt(v, ldv, bsl.reuse, swb_v)
            v.tensor_single_scalar(xsl.ap, xsl.ap, SW, ALU.mult)
            emit_round(v, xsl.ap, scr, scr2, ctx.rR)
            v.tensor_scalar(xsl.ap, xsl.ap, -1.0, 1.0, ALU.max, ALU.min)
            v.drain()
            ctx.ev.inc(v.tensor_copy(bsl.ap, xsl.ap))
        evv = ctx.ev.now()

        @blk.scalar
        def _(s, ck=ck, cm=cm, bsl=bsl, evv=evv):
            wt(s, evv)
            ctx.store(s, WQ[ck * 128:(ck + 1) * 128, cm * 2048:(cm + 1) * 2048],
                      bsl.ap, bsl.ssem)
        stv = bsl.ssem.now()
        xsl.release(stv)
        bsl.release(stv)
    ctx.barrier()


def actquant(ctx, XQ, src, nchunks, SCL_hbm, INVW, extra, stats_hbm=None):
    blk = ctx.blk
    am = ctx.rS[:, 2048:4096]
    xr = DRing(ctx, ctx.rE, 2, 2048, loads="la")
    lc = ctx.dsem("lc")
    so = ctx.dsem("so3")

    if stats_hbm is None:
        @blk.vector
        def _(v):
            ctx.ev.inc(v.memset(am, 0.0))
        for c in range(nchunks):
            xsl = xr.take()

            @blk.sync
            def _(q, c=c, xsl=xsl):
                wt(q, xsl.reuse)
                ctx.load(q, xsl.ap, src[c * 128:(c + 1) * 128, :], xsl.lsem)
            ldv = xsl.lsem.now()

            @blk.vector
            def _(v, xsl=xsl, ldv=ldv):
                wt(v, ldv)
                v.drain()
                v.tensor_tensor(am, am, xsl.ap, ALU.max)
                v.drain()
                ctx.ev.inc(v.scalar_tensor_tensor(am, xsl.ap, -1.0, am,
                                                  ALU.mult, ALU.max))
            xsl.release(ctx.ev.now())
    else:
        @blk.sync
        def _(q):
            ctx.load(q, am, stats_hbm, lc)
        am_ld = lc.now()

        @blk.vector
        def _(v):
            wt(v, am_ld)
            ctx.ev.inc(v.tensor_copy(am, am))
    am_v = ctx.ev.now()

    AMAX = ctx.stats[:, 96:112]
    stg = ctx.sqst[:, 0:128]
    for j in range(16):
        bank = j % 2

        @blk.tensor
        def _(te, j=j, bank=bank, am_v=am_v):
            wt(te, am_v)
            ctx.bankwait(te, bank)
            ctx.mm.inc(te.transpose(ctx.psum[bank][:, 0:128],
                                    am[:, j * 128:(j + 1) * 128], ctx.idn))
        mv = ctx.mm.now()

        @blk.vector
        def _(v, j=j, bank=bank, mv=mv):
            wt(v, mv)
            v.drain()
            v.tensor_copy(stg, ctx.psum[bank][:, 0:128])
            v.drain()
            ctx.ev.inc(v.tensor_reduce(AMAX[:, j:j + 1], stg, AX.X, ALU.max))
        ctx.bankw[bank] = ctx.ev.now()

    SC16 = ctx.stats[:, 112:128]
    IV16 = ctx.stats[:, 128:144]
    iw = ctx.gsmall[:, 8:9]
    lc2 = ctx.dsem("lc2")

    @blk.sync
    def _(q):
        ctx.load(q, iw, INVW, lc2)
    iw_ld = lc2.now()

    @blk.vector
    def _(v):
        wt(v, ctx.ev.now())
        wt(v, iw_ld)
        v.tensor_scalar_max(AMAX, AMAX, 1e-5)
        v.drain()
        v.reciprocal(SC16, AMAX)
        v.tensor_scalar_mul(IV16, AMAX, 1.0 / 127.0)
        v.drain()
        ctx.ev.inc(v.tensor_scalar_mul(SC16, SC16, 127.0))

    SCb = ctx.rS[:, 0:2048]
    IVb = am
    bcast_cols(ctx, [(SC16, SCb), (IV16, IVb)], 16)

    @blk.vector
    def _(v):
        wt(v, ctx.ev.now())
        v.tensor_single_scalar(IVb, IVb, iw, ALU.mult)
        if extra != 1.0:
            v.drain()
            v.tensor_scalar_mul(IVb, IVb, extra)
        ctx.ev.inc(v.drain())
    scl_v = ctx.ev.now()

    @blk.scalar
    def _(s, scl_v=scl_v):
        wt(s, scl_v)
        ctx.store(s, SCL_hbm, IVb, so)

    br = DRing(ctx, ctx.rB.bitcast(BF16), 2, 2048, stores="se")
    scr = ctx.rF[:, 0:2048]
    scr2 = ctx.rF[:, 2048:4096]
    for c in range(nchunks):
        xsl = xr.take()
        bsl = br.take()

        @blk.sync
        def _(q, c=c, xsl=xsl):
            wt(q, xsl.reuse)
            ctx.load(q, xsl.ap, src[c * 128:(c + 1) * 128, :], xsl.lsem)
        ldv = xsl.lsem.now()

        @blk.vector
        def _(v, xsl=xsl, bsl=bsl, ldv=ldv, scl_v=scl_v):
            wt(v, ldv, bsl.reuse, scl_v)
            v.tensor_mul(xsl.ap, xsl.ap, SCb)
            emit_round(v, xsl.ap, scr, scr2, ctx.rR)
            v.tensor_scalar(xsl.ap, xsl.ap, -128.0, 127.0, ALU.max, ALU.min)
            v.drain()
            ctx.ev.inc(v.tensor_copy(bsl.ap, xsl.ap))
        evv = ctx.ev.now()

        @blk.scalar
        def _(s, c=c, bsl=bsl, evv=evv):
            wt(s, evv)
            ctx.store(s, XQ[c * 128:(c + 1) * 128, :], bsl.ap, bsl.ssem)
        stv = bsl.ssem.now()
        xsl.release(stv)
        bsl.release(stv)
    ctx.barrier()


def build_program(n_stages=None, dump=()):
    _WCACHE.clear()
    nc = bass.Bass("TRN2", target_bir_lowering=False)

    def din(name, shape, dt=F32):
        return nc.dram_tensor(name, list(shape), dt, kind="ExternalInput").ap()

    def dsc(name, shape, dt=F32):
        return nc.dram_tensor(name, list(shape), dt, kind="Internal").ap()

    x_in = din("x", [D, T])
    out = nc.dram_tensor("out", [D, T], F32, kind="ExternalOutput").ap()

    W = {}
    for name, shape in [
        ("gwT", [D, NE]), ("guT", [NE, 11, 128, 4096]), ("dnT", [NE, 8, 128, 2816]),
        ("sguT", [22, 128, 4096]), ("sdT", [16, 128, 2816]),
        ("qkv0T", [12, 128, 4096]), ("o0T", [8, 128, 4096]),
        ("qkv1T", [12, 128, 4096]), ("o1T", [8, 128, 4096]),
        ("w1T", [D, FH]), ("w2T", [FH, D]), ("gammaR", [128, 16]),
        ("R0C", [128, T]), ("R0S", [128, T]), ("R1C", [128, T]), ("R1S", [128, T]),
        ("PSW0", [128, 128]), ("PSW1", [128, 128]), ("MASKS", [4, 128, 512]),
        ("IDN", [128, 128]),
    ]:
        W[name] = din(name, shape)

    XR = dsc("XR", [D, T])
    LN = dsc("LNs", [D, T])
    RT = dsc("RT", [D, T])
    GU = dsc("GU", [2 * FS, T])
    HDN = dsc("HDN", [FS, T])
    WB = dsc("WBs", [NE, 128, T])
    QKV = dsc("QKV", [3072, T])
    VT = dsc("VTs", [T, 512])
    YA = dsc("YAs", [D, T])
    RMS = dsc("RMS", [D, T])
    H = dsc("Hs", [FH, T])
    XQ1 = dsc("XQ1", [D, T], BF16)
    XQ2 = dsc("XQ2", [FH, T], BF16)
    W1Q = dsc("W1Q", [D, FH], BF16)
    W2Q = dsc("W2Q", [FH, D], BF16)
    SCLA = dsc("SCLA", [128, T])
    SCLB = dsc("SCLB", [128, T])
    AM2 = dsc("AM2", [128, T])
    INVW1 = dsc("INVW1", [128, 1])
    INVW2 = dsc("INVW2", [128, 1])

    scr = dict(XR=XR, LN=LN, RT=RT, GU=GU, HDN=HDN, WB=WB, QKV=QKV, VT=VT,
               YA=YA, RMS=RMS, H=H, XQ1=XQ1, XQ2=XQ2, W1Q=W1Q, W2Q=W2Q,
               SCLA=SCLA, SCLB=SCLB, AM2=AM2, INVW1=INVW1, INVW2=INVW2)

    import contextlib
    with contextlib.ExitStack() as stack:
        blk = None

        class _C(Ctx):
            def semalloc(self, name):
                return stack.enter_context(nc.semaphore(name))

        h_mm = stack.enter_context(nc.semaphore("mm"))
        h_ev = stack.enter_context(nc.semaphore("ev"))
        h_sc = stack.enter_context(nc.semaphore("sc"))
        blk = stack.enter_context(nc.Block())
        ctx = _C(nc, blk)
        ctx.mm, ctx.ev, ctx.sc = Sem(h_mm), Sem(h_ev), Sem(h_sc)

        A = nc.alloc_sbuf_tensor
        ctx.rA = A("rA", [128, 16384], F32).ap()
        ctx.rB = A("rB", [128, 2048], F32).ap()
        ctx.rW0 = A("rW0", [128, 4096], F32).ap()
        ctx.rW1 = A("rW1", [128, 4096], F32).ap()
        ctx.rE = A("rE", [128, 4096], F32).ap()
        ctx.rF = A("rF", [128, 4096], F32).ap()
        ctx.rS = A("rS", [128, 4096], F32).ap()
        ctx.rC = A("rC", [128, 4096], F32).ap()
        ctx.rR = A("rR", [128, 2048], I32).ap()
        ctx.idn = A("idn", [128, 128], F32).ap()
        ctx.pswap = A("pswap", [128, 128], F32).ap()
        ctx.trow = A("trow", [128, 256], F32).ap()
        ctx.stats = A("stats", [128, 256], F32).ap()
        ctx.sqst = A("sqst", [128, 256], F32).ap()
        ctx.gw_sb = A("gw_sb", [128, 128], F32).ap()
        ctx.ones_col = A("ones_col", [128, 1], F32).ap()
        ctx.ones_row = A("ones_row", [1, 128], F32).ap()
        ctx.gsmall = A("gsmall", [128, 640], F32).ap()
        ctx.gammaR = A("gammaR_sb", [128, 16], F32).ap()
        for i in range(8):
            ctx.psum.append(nc.alloc_psum_tensor(f"p{i}", [128, 512], F32).ap())

        lc = ctx.dsem("lc")

        @blk.sync
        def _(q):
            ctx.load(q, ctx.idn, W["IDN"], lc)
            ctx.load(q, ctx.gammaR, W["gammaR"], lc)
        ctx.gamma_ld = lc.now()

        @blk.vector
        def _(v):
            v.memset(ctx.ones_col, 1.0)
            ctx.ev.inc(v.memset(ctx.ones_row, 1.0))

        @blk.tensor
        def _(te):
            wt(te, ctx.gamma_ld)
            wt(te, ctx.ev.now())

        stages = []

        def stage(fn, *a, **k):
            stages.append((fn, a, k))

        def moe_block(src):
            stage(norm_fm, ctx, LN, src, kind="ln")
            stage(router, ctx, WB, W["gwT"], LN)
            for n in range(NE):
                stage(gemm, ctx, GU, W["guT"][n], LN, D, 2 * FE, T)
                stage(moe_hidden, ctx, HDN, GU, FE, wb_hbm=WB[n])
                stage(gemm, ctx, RT, W["dnT"][n], HDN, FE, D, T,
                      addf=(RT if n > 0 else None))
            stage(gemm, ctx, GU, W["sguT"], LN, D, 2 * FS, T)
            stage(moe_hidden, ctx, HDN, GU, FS)
            stage(gemm, ctx, RT, W["sdT"], HDN, FS, D, T, addf=RT, t_block=512)
            stage(resadd, ctx, XR, src, RT, 0.5)

        def attn_block(layer):
            qkvT = W["qkv0T"] if layer == 0 else W["qkv1T"]
            oT = W["o0T"] if layer == 0 else W["o1T"]
            RCc = W["R0C"] if layer == 0 else W["R1C"]
            RSs = W["R0S"] if layer == 0 else W["R1S"]
            PSW = W["PSW0"] if layer == 0 else W["PSW1"]
            sm = SM0 if layer == 0 else SM1
            nh, nkv, hd = (NH0, KV0, HD0) if layer == 0 else (NH1, KV1, HD1)
            stage(norm_fm, ctx, LN, XR, kind="ln")
            stage(gemm, ctx, QKV, qkvT, LN, D, 3072, T, scale_mtiles=(16, sm))
            stage(rope, ctx, QKV, 16, RCc, RSs, PSW, row0=0)
            stage(rope, ctx, QKV, 4, RCc, RSs, PSW, row0=2048)
            stage(transpose_fm, ctx, VT, QKV[2560:3072, :], 512, T)
            stage(attention, ctx, YA, QKV, VT, W["MASKS"],
                  nh=nh, nkv=nkv, hd=hd, causal=(layer == 0))
            stage(gemm, ctx, XR, oT, YA, D, D, T, addf=XR)

        def bitffn_block(first, last):
            stage(norm_fm, ctx, LN, XR, kind="ln")
            stage(norm_fm, ctx, RMS, LN, kind="rms", eps=1e-6, gamma=ctx.gammaR)
            if first:
                stage(weight_quant, ctx, W1Q, W["w1T"], D, FH, INVW1)
                stage(weight_quant, ctx, W2Q, W["w2T"], FH, D, INVW2)
            stage(actquant, ctx, XQ1, RMS, 16, SCLA, INVW1, 1.0)
            stage(gemm, ctx, H, W1Q, XQ1, D, FH, T, dt=BF16, mul_hbm=SCLA,
                  act=AF.Gelu, stats=ctx.rS[:, 2048:4096], stats_hbm=AM2,
                  t_block=2048)
            stage(actquant, ctx, XQ2, H, 64, SCLB, INVW2, 0.5, stats_hbm=AM2)
            stage(gemm, ctx, (out if last else XR), W2Q, XQ2, FH, D, T,
                  dt=BF16, mul_hbm=SCLB, addf=XR, t_block=512)

        moe_block(x_in)
        attn_block(0)
        moe_block(XR)
        bitffn_block(first=True, last=False)
        attn_block(1)
        bitffn_block(first=False, last=True)

        for i, (fn, a, k) in enumerate(stages):
            if n_stages is not None and i >= n_stages:
                break
            ctx.begin()
            fn(*a, **k)

        if dump:
            for name in dump:
                ap = scr[name]
                dbg = nc.dram_tensor("dbg_" + name, list(ap.shape), ap.dtype,
                                     kind="ExternalOutput").ap()

                @blk.sync
                def _(q, ap=ap, dbg=dbg):
                    wt(q, ctx.barwaits)
                    ctx.store(q, dbg, ap, ctx.dsem("so0"))

        final = list(ctx.barwaits) + [(s, n) for s, n in ctx.dirty.items()]

        @blk.sync
        def _(q):
            wt(q, final)
    return nc


def stage_inputs(inputs):
    g = {k: np.asarray(v) for k, v in inputs.items()}
    f = np.float32
    ct = np.ascontiguousarray

    def TT(a):
        return ct(np.asarray(a).T)

    m = {}
    m["gwT"] = TT(g["L0_ffn_gate_weight"])
    m["guT"] = ct(np.concatenate([
        np.transpose(g["L0_ffn_gate_w"], (0, 2, 1)),
        np.transpose(g["L0_ffn_up_w"], (0, 2, 1))], axis=2))
    m["dnT"] = ct(np.transpose(g["L0_ffn_down_w"], (0, 2, 1)))
    m["sguT"] = ct(np.concatenate([TT(g["L0_ffn_sh_gate_w"]),
                                   TT(g["L0_ffn_sh_up_w"])], axis=1))
    m["sdT"] = TT(g["L0_ffn_sh_down_w"])
    m["qkv0T"] = ct(np.concatenate([TT(g["L0_attn_wq"]), TT(g["L0_attn_wk"]),
                                    TT(g["L0_attn_wv"])], axis=1))
    m["o0T"] = TT(g["L0_attn_wo"])
    m["qkv1T"] = ct(np.concatenate([TT(g["L1_attn_Wq"]), TT(g["L1_attn_Wk"]),
                                    TT(g["L1_attn_Wv"])], axis=1))
    m["o1T"] = TT(g["L1_attn_Wo"])
    m["w1T"] = TT(g["L1_ffn_w1"])
    m["w2T"] = TT(g["L1_ffn_w2"])
    m["gammaR"] = ct(np.asarray(g["L1_ffn_gamma"], f).reshape(16, 128).T)

    def group_w(w, esz=4):
        K, M = w.shape
        KC = K // 128
        wcap = 4096 * 4 // esz
        GM = 2 if KC * 256 <= wcap else 1
        gw = GM * 128
        NG = M // gw
        blocks = []
        for gg in range(NG):
            b = w[:, gg * gw:(gg + 1) * gw].reshape(KC, 128, gw)
            blocks.append(np.transpose(b, (1, 0, 2)).reshape(128, KC * gw))
        return ct(np.stack(blocks, axis=0))

    for k in ("qkv0T", "o0T", "qkv1T", "o1T", "sguT", "sdT"):
        m[k] = group_w(m[k])
    m["guT"] = ct(np.stack([group_w(m["guT"][n]) for n in range(NE)], axis=0))
    m["dnT"] = ct(np.stack([group_w(m["dnT"][n]) for n in range(NE)], axis=0))

    t = np.arange(T, dtype=np.float64)
    d = np.arange(128)
    th0 = 1.0 / (10000.0 ** (np.arange(0, HD0, 2, dtype=np.float64) / HD0))
    fr0 = th0[:, None] * t[None, :]
    i0 = (d % 64) // 2
    sgn0 = np.where(d % 2 == 0, -1.0, 1.0)
    m["R0C"] = ct(np.cos(fr0)[i0, :].astype(f))
    m["R0S"] = ct((sgn0[:, None] * np.sin(fr0)[i0, :]).astype(f))
    th1 = 1.0 / (10000.0 ** (np.arange(0, HD1, 2, dtype=np.float64) / HD1))
    fr1 = th1[:, None] * t[None, :]
    i1 = d % 64
    sgn1 = np.where(d < 64, -1.0, 1.0)
    m["R1C"] = ct(np.cos(fr1)[i1, :].astype(f))
    m["R1S"] = ct((sgn1[:, None] * np.sin(fr1)[i1, :]).astype(f))

    p0 = np.zeros((128, 128), f)
    p0[np.arange(128), np.arange(128) ^ 1] = 1.0
    m["PSW0"] = p0
    p1 = np.zeros((128, 128), f)
    p1[np.arange(128), np.arange(128) ^ 64] = 1.0
    m["PSW1"] = p1

    masks = np.zeros((4, 128, 512), f)
    p = np.arange(128)[:, None]
    c = np.arange(512)[None, :]
    for a_ in range(4):
        masks[a_] = np.where(c <= 128 * a_ + p, 0.0, NEGBIG)
    m["MASKS"] = masks
    m["IDN"] = np.eye(128, dtype=f)
    for k in m:
        assert m[k].dtype == np.float32, (k, m[k].dtype)
    return m


def kernel(**inputs):
    m = stage_inputs(inputs)
    x = np.asarray(inputs["L0_attn_x"])
    nc = build_program(n_stages=N_STAGES, dump=tuple(DUMP))
    in_maps = []
    for b in range(8):
        d = dict(m)
        d["x"] = np.ascontiguousarray(x[b].T)
        in_maps.append(d)
    res = run_bass_kernel_spmd(nc, in_maps, core_ids=list(range(8)))
    outs = [np.asarray(r["out"]).T for r in res.results]
    return np.stack(outs, axis=0)


# version 07
# speedup vs baseline: 1.0272x; 1.0272x over previous
import math
import os
import numpy as np
import concourse.bass as bass
import concourse.mybir as mybir
from concourse.bass_utils import run_bass_kernel_spmd

os.environ.setdefault("GAUGE_NTFF_IGNORE_DMA", "1")

F32 = mybir.dt.float32
BF16 = mybir.dt.bfloat16
I32 = mybir.dt.int32
AX = mybir.AxisListType
ALU = mybir.AluOpType
AF = mybir.ActivationFunctionType

D = 2048
T = 2048
NE = 8
FE = 1408
FS = 2816
FH = 8192
NH0, KV0, HD0 = 32, 8, 64
NH1, KV1, HD1 = 16, 4, 128
SM0 = 1.0 / math.sqrt(HD0)
SM1 = 1.0 / math.sqrt(HD1)
NEGBIG = -1e30

N_STAGES = None
DUMP = []


class Sem:
    def __init__(self, handle):
        self.h = handle
        self.n = 0

    def inc(self, ins, by=1):
        ins.then_inc(self.h, by)
        self.n += by
        return self.n

    def now(self):
        return (self, self.n)


_WCACHE = {}


def wt(eng, *waits):
    key0 = eng.engine

    def one(w):
        k = (key0, id(w[0]))
        if _WCACHE.get(k, -1) >= w[1]:
            return
        _WCACHE[k] = w[1]
        eng.wait_ge(w[0].h, w[1])

    for w in waits:
        if w is None:
            continue
        if isinstance(w, list):
            for x in w:
                if x is not None:
                    one(x)
        else:
            one(w)


class DRing:
    def __init__(self, ctx, ap, nslots, width, loads=None, stores=None):
        self.ap = ap
        self.ns = nslots
        self.w = width
        self.ls = [ctx.dsem(f"{loads}{i}") for i in range(nslots)] if loads else None
        self.ss = [ctx.dsem(f"{stores}{i}") for i in range(nslots)] if stores else None
        self.reuse = [[] for _ in range(nslots)]
        self.i = 0

    def take(self):
        s = self.i % self.ns
        self.i += 1
        return RSlot(self, s)

    def release(self, s, *semvals):
        self.reuse[s] = [sv for sv in semvals if sv is not None]


class RSlot:
    def __init__(self, ring, s):
        self.ring = ring
        self.s = s
        self.ap = ring.ap[:, s * ring.w:(s + 1) * ring.w]
        self.lsem = ring.ls[s] if ring.ls else None
        self.ssem = ring.ss[s] if ring.ss else None
        self.reuse = list(ring.reuse[s])

    def release(self, *semvals):
        self.ring.release(self.s, *semvals)


class Ctx:
    def __init__(self, nc, blk):
        self.nc = nc
        self.blk = blk
        self.psum = []
        self.bankw = [None] * 8
        self.dsems = {}
        self.dirty = {}
        self.barwaits = []

    def dsem(self, name):
        if name not in self.dsems:
            self.dsems[name] = Sem(self.semalloc(name))
        return self.dsems[name]

    def load(self, q, out, in_, sem):
        sem.inc(q.dma_start(out=out, in_=in_), 16)

    def store(self, q, out, in_, sem):
        sem.inc(q.dma_start(out=out, in_=in_), 16)
        self.dirty[sem] = sem.n

    def barrier(self):
        self.barwaits = [(s, n) for s, n in self.dirty.items()]
        self.dirty = {}
        self.bankw = [None] * 8

    def begin(self):
        b = list(self.barwaits)
        blk = self.blk

        @blk.sync
        def _(q):
            wt(q, b)

        @blk.tensor
        def _(te):
            wt(te, b)

        @blk.vector
        def _(v):
            wt(v, b)

        @blk.scalar
        def _(s):
            wt(s, b)

    def bankwait(self, eng, bank):
        w = self.bankw[bank]
        if w is not None:
            wt(eng, w)


def emit_round(v, y, scr, scr2, i32t):
    W = y.shape[1]
    i32v = i32t[:, 0:W]
    v.drain()
    v.tensor_copy(i32v, y)
    v.drain()
    v.tensor_copy(scr, i32v)
    v.drain()
    v.tensor_sub(y, y, scr)
    v.drain()
    v.tensor_single_scalar(scr2, y, 0.5, ALU.is_ge)
    v.drain()
    v.tensor_add(scr, scr, scr2)
    v.tensor_single_scalar(y, y, -0.5, ALU.is_le)
    v.drain()
    v.tensor_sub(y, scr, y)
    v.drain()


def gemm(ctx, out, wT, rhs, K, M, N, *,
         dt=F32, scale=None, scale_mtiles=None, mul_hbm=None, act=None,
         addf=None, stats=None, stats_hbm=None, t_block=1024, rhs_fuse=None):
    blk = ctx.blk
    esz = 4 if dt == F32 else 2
    KC = K // 128
    MT = M // 128
    TB = t_block
    NB = N // TB
    NT = TB // 512
    wcap = 4096 * 4 // esz
    GM = 2 if KC * 256 <= wcap else 1
    NG = (MT + GM - 1) // GM
    assert KC * TB * esz <= 65536, "rhs slab too large"
    assert KC * GM * 128 <= wcap, "weight slab too large"

    slab = ctx.rA if dt == F32 else ctx.rA.bitcast(BF16)
    wbufs = [ctx.rW0, ctx.rW1] if dt == F32 else \
            [ctx.rW0.bitcast(BF16), ctx.rW1.bitcast(BF16)]
    lda = ctx.dsem("lda")
    lw = [ctx.dsem("lw0"), ctx.dsem("lw1")]
    er = DRing(ctx, ctx.rE, 8, 512, stores="se")
    fr = DRing(ctx, ctx.rF, 8, 512, loads="lf") if addf is not None else None
    mulsb = ctx.rS[:, 0:N] if mul_hbm is not None else None
    if mul_hbm is not None:
        @blk.sync
        def _(q):
            ctx.load(q, mulsb, mul_hbm, lda)
    if stats is not None:
        @blk.vector
        def _(v):
            ctx.ev.inc(v.memset(stats, 0.0))

    if rhs_fuse is not None:
        GUa, FF, wb_hbm = rhs_fuse
        lc = ctx.dsem("lc")
        if wb_hbm is not None:
            @blk.sync
            def _(q):
                ctx.load(q, ctx.rS[:, 2048:4096], wb_hbm, lc)
        wb_ld = lc.now()
        hg = DRing(ctx, ctx.rC, 4, 1024, loads="la")
        hu = DRing(ctx, ctx.rB, 2, 1024, loads="lb")

    for tb in range(NB):
        mm_before = ctx.mm.n

        if rhs_fuse is None:
            @blk.sync
            def _(q, tb=tb):
                if tb > 0:
                    wt(q, (ctx.mm, mm_before))
                for c in range(KC):
                    ctx.load(q, slab[:, c * TB:(c + 1) * TB],
                             rhs[c * 128:(c + 1) * 128, tb * TB:(tb + 1) * TB],
                             lda)
            rhs_ld = lda.now()
        else:
            for c in range(KC):
                gsl = hg.take()
                usl = hu.take()

                ga = gsl.ap[:, 0:TB]
                ua = usl.ap[:, 0:TB]

                @blk.sync
                def _(q, c=c, tb=tb, gsl=gsl, usl=usl, ga=ga, ua=ua):
                    if c == 0 and tb > 0:
                        wt(q, (ctx.mm, mm_before))
                    wt(q, gsl.reuse, usl.reuse)
                    ctx.load(q, ga,
                             GUa[c * 128:(c + 1) * 128, tb * TB:(tb + 1) * TB],
                             gsl.lsem)
                    ctx.load(q, ua,
                             GUa[FF + c * 128: FF + (c + 1) * 128,
                                 tb * TB:(tb + 1) * TB], usl.lsem)
                lg, lu = gsl.lsem.now(), usl.lsem.now()

                @blk.scalar
                def _(s, ga=ga):
                    wt(s, lg)
                    ctx.sc.inc(s.activation(ga, ga, AF.Silu))
                scv = ctx.sc.now()

                @blk.vector
                def _(v, c=c, tb=tb, ga=ga, ua=ua):
                    wt(v, scv, lu, wb_ld)
                    if tb > 0:
                        wt(v, (ctx.mm, mm_before))
                    sl = slab[:, c * TB:(c + 1) * TB]
                    if wb_hbm is not None:
                        v.tensor_mul(ua, ga, ua)
                        v.drain()
                        ctx.ev.inc(v.tensor_mul(
                            sl, ua,
                            ctx.rS[:, 2048 + tb * TB: 2048 + (tb + 1) * TB]))
                    else:
                        ctx.ev.inc(v.tensor_mul(sl, ga, ua))
                evv = ctx.ev.now()
                gsl.release(evv)
                usl.release(evv)
            rhs_ld = None
            fill_v = ctx.ev.now()

        wgrp_after = {}
        for g in range(NG):
            mts = list(range(g * GM, min(g * GM + GM, MT)))
            gw = len(mts) * 128
            wb = wbufs[g % 2]
            ws = lw[g % 2]

            grouped = len(wT.shape) == 3

            @blk.sync
            def _(q, g=g, mts=mts, gw=gw, wb=wb, ws=ws, tb=tb, grouped=grouped):
                if g >= 2:
                    wt(q, (ctx.mm, wgrp_after[g - 2]))
                elif tb > 0:
                    wt(q, (ctx.mm, mm_before))
                if grouped:
                    ctx.load(q, wb[:, 0:KC * gw], wT[g], ws)
                else:
                    for c in range(KC):
                        ctx.load(q, wb[:, c * gw:(c + 1) * gw],
                                 wT[c * 128:(c + 1) * 128,
                                    mts[0] * 128:mts[0] * 128 + gw], ws)
            w_ld = ws.now()

            for mi, mt in enumerate(mts):
                for t in range(NT):
                    tg = tb * NT + t
                    bank = (mt * NT + t) % 4
                    pb = ctx.psum[bank]

                    @blk.tensor
                    def _(te, mi=mi, t=t, bank=bank, pb=pb, gw=gw, wb=wb):
                        if rhs_fuse is None:
                            wt(te, rhs_ld, w_ld)
                        else:
                            wt(te, fill_v, w_ld)
                        ctx.bankwait(te, bank)
                        ins = None
                        for c in range(KC):
                            ins = te.matmul(
                                pb,
                                wb[:, c * gw + mi * 128: c * gw + (mi + 1) * 128],
                                slab[:, c * TB + t * 512: c * TB + t * 512 + 512],
                                start=(c == 0), stop=(c == KC - 1))
                        ctx.mm.inc(ins)
                    mm_v = ctx.mm.now()

                    out_ap = out[mt * 128:(mt + 1) * 128, tg * 512:(tg + 1) * 512]
                    esl = er.take()
                    et = esl.ap
                    sc_imm = scale
                    if scale_mtiles is not None and sc_imm is None:
                        sc_imm = scale_mtiles[1] if mt < scale_mtiles[0] else None

                    fsl = f_ld = None
                    if addf is not None:
                        fsl = fr.take()

                        @blk.sync
                        def _(q, fsl=fsl, mt=mt, tg=tg):
                            wt(q, fsl.reuse)
                            ctx.load(q, fsl.ap, addf[
                                mt * 128:(mt + 1) * 128,
                                tg * 512:(tg + 1) * 512], fsl.lsem)
                        f_ld = fsl.lsem.now()

                    prod = None
                    if mul_hbm is not None:
                        @blk.vector
                        def _(v, et=et, pb=pb, tg=tg, esl=esl):
                            wt(v, mm_v, esl.reuse)
                            if rhs_ld is not None:
                                wt(v, rhs_ld)
                            ctx.ev.inc(v.tensor_mul(
                                et, pb, mulsb[:, tg * 512:(tg + 1) * 512]))
                        ctx.bankw[bank] = ctx.ev.now()
                        prod = ctx.ev.now()

                    if act is not None or prod is None:
                        @blk.scalar
                        def _(s, et=et, pb=pb, prod=prod, esl=esl, sc_imm=sc_imm):
                            if prod is not None:
                                wt(s, prod)
                                src = et
                            else:
                                wt(s, mm_v, esl.reuse)
                                src = pb
                            if act is not None:
                                ctx.sc.inc(s.activation(
                                    et, src, act,
                                    scale=1.0 if sc_imm is None else sc_imm))
                            elif sc_imm is not None:
                                ctx.sc.inc(s.mul(et, src, sc_imm))
                            else:
                                ctx.sc.inc(s.copy(et, src))
                        if prod is None:
                            ctx.bankw[bank] = ctx.sc.now()
                        prod = ctx.sc.now()
                    elif sc_imm is not None:
                        @blk.vector
                        def _(v, et=et, sc_imm=sc_imm):
                            v.drain()
                            ctx.ev.inc(v.tensor_scalar_mul(et, et, sc_imm))
                        prod = ctx.ev.now()

                    stat_sem = None
                    if stats is not None:
                        @blk.vector
                        def _(v, et=et, tg=tg, prod=prod):
                            wt(v, prod)
                            stt = stats[:, tg * 512:(tg + 1) * 512]
                            v.drain()
                            v.tensor_tensor(stt, stt, et, ALU.max)
                            v.drain()
                            ctx.ev.inc(v.scalar_tensor_tensor(
                                stt, et, -1.0, stt, ALU.mult, ALU.max))
                        stat_sem = ctx.ev.now()

                    if addf is not None:
                        @blk.vector
                        def _(v, et=et, fsl=fsl, prod=prod, f_ld=f_ld):
                            if prod[0] is not ctx.ev:
                                wt(v, prod)
                            else:
                                v.drain()
                            wt(v, f_ld)
                            ctx.ev.inc(v.tensor_add(et, et, fsl.ap))
                        prod = ctx.ev.now()

                    @blk.scalar
                    def _(s, et=et, out_ap=out_ap, prod=prod, esl=esl):
                        wt(s, prod)
                        ctx.store(s, out_ap, et, esl.ssem)
                    stv = esl.ssem.now()
                    esl.release(stv, stat_sem)
                    if addf is not None:
                        fsl.release(stv)
            wgrp_after[g] = ctx.mm.n

    if stats is not None and stats_hbm is not None:
        ev_fin = ctx.ev.now()
        so = ctx.dsem("so0")

        @blk.scalar
        def _(s):
            wt(s, ev_fin)
            ctx.store(s, stats_hbm, stats, so)
    ctx.barrier()


def bcast_cols(ctx, pairs, nj, extra_wait=None):
    blk = ctx.blk
    ev0 = ctx.ev.n
    for i, (src, dst) in enumerate(pairs):
        for j in range(nj):
            k = (i * nj + j) % 2
            bankT = 6 + k
            bankB = 4 + k
            tr = ctx.trow[0:1, k * 128:(k + 1) * 128]

            @blk.tensor
            def _(te, src=src, j=j, bankT=bankT):
                wt(te, (ctx.ev, ev0))
                ctx.bankwait(te, bankT)
                ctx.mm.inc(te.matmul(ctx.psum[bankT][0:1, 0:128],
                                     src[:, j:j + 1], ctx.idn,
                                     start=True, stop=True))
            mv1 = ctx.mm.now()

            @blk.vector
            def _(v, tr=tr, bankT=bankT, mv1=mv1):
                wt(v, mv1)
                ctx.ev.inc(v.tensor_copy(tr, ctx.psum[bankT][0:1, 0:128]))
            ctx.bankw[bankT] = ctx.ev.now()
            trv = ctx.ev.now()

            @blk.tensor
            def _(te, tr=tr, bankB=bankB, trv=trv):
                wt(te, trv)
                ctx.bankwait(te, bankB)
                ctx.mm.inc(te.matmul(ctx.psum[bankB][:, 0:128], ctx.ones_row,
                                     tr, start=True, stop=True))
            mv2 = ctx.mm.now()

            @blk.vector
            def _(v, j=j, dst=dst, bankB=bankB, mv2=mv2):
                wt(v, mv2, extra_wait)
                ctx.ev.inc(v.tensor_copy(dst[:, j * 128:(j + 1) * 128],
                                         ctx.psum[bankB][:, 0:128]))
            ctx.bankw[bankB] = ctx.ev.now()


def norm_fm(ctx, out, in_, *, kind="ln", eps=1e-5, gamma=None):
    blk = ctx.blk
    TB = 1024
    slab = ctx.rA
    lda = ctx.dsem("lda")
    st = ctx.stats
    S16, Q16 = st[:, 0:16], st[:, 16:32]
    NM16, RS16 = st[:, 32:48], st[:, 48:64]
    VP16, SD16 = st[:, 64:80], st[:, 80:96]
    er = DRing(ctx, ctx.rE, 4, 1024, stores="se")

    for tb in range(2):
        prev_stores = [(s, n) for s, n in ctx.dirty.items()]

        @blk.sync
        def _(q, tb=tb):
            if tb > 0:
                wt(q, prev_stores)
            for c in range(16):
                ctx.load(q, slab[:, c * TB:(c + 1) * TB],
                         in_[c * 128:(c + 1) * 128, tb * TB:(tb + 1) * TB], lda)
        ld_v = lda.now()

        sq_free = [None, None]
        for j in range(8):
            bs, bq = j % 2, 2 + (j % 2)
            for c in range(16):
                sq = ctx.sqst[:, (c % 2) * 128:(c % 2) * 128 + 128]

                @blk.vector
                def _(v, c=c, j=j, sq=sq):
                    wt(v, ld_v, sq_free[c % 2])
                    xs = slab[:, c * TB + j * 128: c * TB + j * 128 + 128]
                    ctx.ev.inc(v.tensor_mul(sq, xs, xs))
                sc_v = ctx.ev.now()

                @blk.tensor
                def _(te, c=c, j=j, bs=bs, bq=bq, sq=sq):
                    wt(te, sc_v)
                    if c == 0:
                        ctx.bankwait(te, bs)
                        ctx.bankwait(te, bq)
                    if kind == "ln":
                        te.matmul(ctx.psum[bs][:, 0:1],
                                  slab[:, c * TB + j * 128: c * TB + j * 128 + 128],
                                  ctx.ones_col, start=(c == 0), stop=(c == 15))
                    ctx.mm.inc(te.matmul(ctx.psum[bq][:, 0:1], sq, ctx.ones_col,
                                         start=(c == 0), stop=(c == 15)))
                sq_free[c % 2] = ctx.mm.now()
            mm_v = ctx.mm.now()

            @blk.vector
            def _(v, j=j, bs=bs, bq=bq):
                wt(v, mm_v)
                if kind == "ln":
                    v.tensor_copy(S16[:, j:j + 1], ctx.psum[bs][:, 0:1])
                ctx.ev.inc(v.tensor_copy(Q16[:, j:j + 1], ctx.psum[bq][:, 0:1]))
            ctx.bankw[bs] = ctx.ev.now()
            ctx.bankw[bq] = ctx.ev.now()

        @blk.vector
        def _(v):
            wt(v, ctx.ev.now())
            DR = v.drain
            S8, Q8 = S16[:, 0:8], Q16[:, 0:8]
            NM8, VP8, SD8 = NM16[:, 0:8], VP16[:, 0:8], SD16[:, 0:8]
            if kind == "ln":
                v.tensor_scalar_mul(NM8, S8, -1.0 / 2048.0)
                v.tensor_scalar_mul(VP8, Q8, 1.0 / 2048.0)
                DR()
                v.tensor_mul(SD8, NM8, NM8)
                DR()
                v.tensor_sub(VP8, VP8, SD8)
                DR()
                ctx.ev.inc(v.tensor_scalar_add(VP8, VP8, eps))
            else:
                ctx.ev.inc(v.tensor_scalar(VP8, Q8, 1.0 / 2048.0, eps,
                                           ALU.mult, ALU.add))
        evv = ctx.ev.now()

        @blk.scalar
        def _(s):
            wt(s, evv)
            ctx.sc.inc(s.activation(SD16[:, 0:8], VP16[:, 0:8], AF.Sqrt))
        scv = ctx.sc.now()

        @blk.vector
        def _(v):
            wt(v, scv)
            v.reciprocal(RS16[:, 0:8], SD16[:, 0:8])
            v.drain()
            ctx.ev.inc(v.tensor_copy(RS16[:, 0:8], RS16[:, 0:8]))

        NMb = ctx.rS[:, 0:1024]
        RSb = ctx.rS[:, 1024:2048]
        if kind == "ln":
            bcast_cols(ctx, [(NM16[:, 0:8], NMb), (RS16[:, 0:8], RSb)], 8)
        else:
            bcast_cols(ctx, [(RS16[:, 0:8], RSb)], 8)
        bc_v = ctx.ev.now()

        for c in range(16):
            esl = er.take()
            et = esl.ap

            @blk.vector
            def _(v, c=c, et=et, esl=esl):
                wt(v, bc_v, esl.reuse)
                xc = slab[:, c * TB:(c + 1) * TB]
                if kind == "ln":
                    v.tensor_add(et, xc, NMb)
                    v.drain()
                    ctx.ev.inc(v.tensor_mul(et, et, RSb))
                else:
                    wt(v, ctx.gamma_ld)
                    v.tensor_mul(et, xc, RSb)
                    v.drain()
                    ctx.ev.inc(v.tensor_single_scalar(
                        et, et, gamma[:, c:c + 1], ALU.mult))
            ev_v = ctx.ev.now()

            @blk.scalar
            def _(s, c=c, et=et, ev_v=ev_v, tb=tb, esl=esl):
                wt(s, ev_v)
                ctx.store(s, out[c * 128:(c + 1) * 128, tb * TB:(tb + 1) * TB],
                          et, esl.ssem)
            esl.release(esl.ssem.now())
    ctx.barrier()


def resadd(ctx, out, a, b, half=0.5):
    blk = ctx.blk
    ar = DRing(ctx, ctx.rE, 2, 2048, loads="la", stores="se")
    br = DRing(ctx, ctx.rF, 2, 2048, loads="lb")
    for c in range(16):
        asl = ar.take()
        bsl = br.take()

        @blk.sync
        def _(q, c=c, asl=asl, bsl=bsl):
            wt(q, asl.reuse, bsl.reuse)
            ctx.load(q, asl.ap, a[c * 128:(c + 1) * 128, :], asl.lsem)
            ctx.load(q, bsl.ap, b[c * 128:(c + 1) * 128, :], bsl.lsem)
        la, lb = asl.lsem.now(), bsl.lsem.now()

        @blk.vector
        def _(v, asl=asl, bsl=bsl):
            wt(v, la, lb)
            ctx.ev.inc(v.scalar_tensor_tensor(asl.ap, bsl.ap, half, asl.ap,
                                              ALU.mult, ALU.add))
        evv = ctx.ev.now()

        @blk.scalar
        def _(s, c=c, asl=asl, evv=evv):
            wt(s, evv)
            ctx.store(s, out[c * 128:(c + 1) * 128, :], asl.ap, asl.ssem)
        stv = asl.ssem.now()
        asl.release(stv)
        bsl.release(stv)
    ctx.barrier()


def rope(ctx, buf, nchunks, cos_hbm, sin_hbm, pswap_hbm, row0=0):
    blk = ctx.blk
    C = ctx.rC[:, 0:2048]
    S = ctx.rC[:, 2048:4096]
    lc = ctx.dsem("lc")

    @blk.sync
    def _(q):
        ctx.load(q, C, cos_hbm, lc)
        ctx.load(q, S, sin_hbm, lc)
        ctx.load(q, ctx.pswap, pswap_hbm, lc)
    cld = lc.now()

    xr = DRing(ctx, ctx.rE, 2, 2048, loads="la")
    orr = DRing(ctx, ctx.rF, 2, 2048, stores="se")
    for c in range(nchunks):
        xsl = xr.take()
        osl = orr.take()
        rows = buf[row0 + c * 128: row0 + (c + 1) * 128, :]

        @blk.sync
        def _(q, rows=rows, xsl=xsl, osl=osl):
            wt(q, cld, xsl.reuse, osl.reuse)
            ctx.load(q, xsl.ap, rows, xsl.lsem)
        ldv = xsl.lsem.now()

        for t in range(4):
            bank = t % 2
            sl = slice(t * 512, (t + 1) * 512)

            @blk.tensor
            def _(te, xsl=xsl, sl=sl, bank=bank, ldv=ldv):
                wt(te, ldv, cld)
                ctx.bankwait(te, bank)
                ctx.mm.inc(te.matmul(ctx.psum[bank], ctx.pswap, xsl.ap[:, sl],
                                     start=True, stop=True))
            mmv = ctx.mm.now()

            @blk.vector
            def _(v, xsl=xsl, osl=osl, sl=sl, bank=bank, mmv=mmv):
                wt(v, mmv)
                v.tensor_mul(osl.ap[:, sl], xsl.ap[:, sl], C[:, sl])
                v.drain()
                v.tensor_mul(xsl.ap[:, sl], ctx.psum[bank], S[:, sl])
                v.drain()
                ctx.ev.inc(v.tensor_add(osl.ap[:, sl], osl.ap[:, sl],
                                        xsl.ap[:, sl]))
            ctx.bankw[bank] = ctx.ev.now()
        evv = ctx.ev.now()

        @blk.scalar
        def _(s, rows=rows, osl=osl, evv=evv):
            wt(s, evv)
            ctx.store(s, rows, osl.ap, osl.ssem)
        stv = osl.ssem.now()
        xsl.release(stv)
        osl.release(stv)
    ctx.barrier()


def transpose_fm(ctx, out, in_, M, N):
    blk = ctx.blk
    xr = DRing(ctx, ctx.rE, 2, 2048, loads="la")
    orr = DRing(ctx, ctx.rF, 8, 512, stores="se")
    for c in range(M // 128):
        xsl = xr.take()

        @blk.sync
        def _(q, c=c, xsl=xsl):
            wt(q, xsl.reuse)
            ctx.load(q, xsl.ap[:, 0:N], in_[c * 128:(c + 1) * 128, :], xsl.lsem)
        ldv = xsl.lsem.now()

        last = None
        for b in range(N // 128):
            bank = b % 2
            osl = orr.take()

            @blk.tensor
            def _(te, b=b, xsl=xsl, bank=bank, ldv=ldv):
                wt(te, ldv)
                ctx.bankwait(te, bank)
                ctx.mm.inc(te.transpose(ctx.psum[bank][:, 0:128],
                                        xsl.ap[:, b * 128:(b + 1) * 128],
                                        ctx.idn))
            mmv = ctx.mm.now()

            @blk.vector
            def _(v, osl=osl, bank=bank, mmv=mmv):
                wt(v, mmv, osl.reuse)
                ctx.ev.inc(v.tensor_copy(osl.ap[:, 0:128],
                                         ctx.psum[bank][:, 0:128]))
            ctx.bankw[bank] = ctx.ev.now()
            evv = ctx.ev.now()

            @blk.scalar
            def _(s, b=b, c=c, osl=osl, evv=evv):
                wt(s, evv)
                ctx.store(s, out[b * 128:(b + 1) * 128, c * 128:(c + 1) * 128],
                          osl.ap[:, 0:128], osl.ssem)
            osl.release(osl.ssem.now())
            last = osl.ssem.now()
        xsl.release(last)
    ctx.barrier()


def router(ctx, WB, gwT, LN):
    blk = ctx.blk
    TB = 1024
    slab = ctx.rA
    gw = ctx.gw_sb
    lda = ctx.dsem("lda")
    lc = ctx.dsem("lc")
    so = ctx.dsem("so1")

    @blk.sync
    def _(q):
        for c in range(16):
            ctx.load(q, gw[:, c * 8:(c + 1) * 8], gwT[c * 128:(c + 1) * 128, :], lc)
    gw_ld = lc.now()

    g = ctx.gsmall
    LT = g[:, 0:64]
    ET = g[:, 64:128]
    M8 = g[:, 128:192]
    W1 = g[:, 192:256]
    M2 = g[:, 256:320]
    GS = g[:, 320:352]
    GS2 = g[:, 352:384]
    E1 = g[:, 384:416]
    E2 = g[:, 416:448]
    GSEL = g[:, 448:480]
    NM = g[:, 480:496]
    SU = g[:, 496:512]
    RV = g[:, 512:528]
    G1 = g[:, 528:544]
    T1 = g[:, 544:560]
    LG = ctx.rC[:, 0:1024]

    for tb in range(2):
        prev_stores = [(s, n) for s, n in ctx.dirty.items()]

        @blk.sync
        def _(q, tb=tb):
            wt(q, gw_ld)
            if tb > 0:
                wt(q, prev_stores)
            for c in range(16):
                ctx.load(q, slab[:, c * TB:(c + 1) * TB],
                         LN[c * 128:(c + 1) * 128, tb * TB:(tb + 1) * TB], lda)
        ld_v = lda.now()

        for t in range(2):
            bank = t % 2

            @blk.tensor
            def _(te, t=t, bank=bank):
                wt(te, ld_v, gw_ld)
                ctx.bankwait(te, bank)
                ins = None
                for c in range(16):
                    ins = te.matmul(ctx.psum[bank][0:8, :],
                                    gw[:, c * 8:(c + 1) * 8],
                                    slab[:, c * TB + t * 512: c * TB + t * 512 + 512],
                                    start=(c == 0), stop=(c == 15))
                ctx.mm.inc(ins)
            mmv = ctx.mm.now()

            @blk.vector
            def _(v, t=t, bank=bank, mmv=mmv):
                wt(v, mmv)
                ctx.ev.inc(v.tensor_copy(LG[0:8, t * 512:(t + 1) * 512],
                                         ctx.psum[bank][0:8, :]))
            ctx.bankw[bank] = ctx.ev.now()
        lg_v = ctx.ev.now()

        for j in range(8):
            LTj = LT[:, j * 8:(j + 1) * 8]
            ETj = ET[:, j * 8:(j + 1) * 8]
            M8j = M8[:, j * 8:(j + 1) * 8]
            W1j = W1[:, j * 8:(j + 1) * 8]
            M2j = M2[:, j * 8:(j + 1) * 8]
            GSj = GS[:, j * 4:(j + 1) * 4]
            GS2j = GS2[:, j * 4:(j + 1) * 4]
            E1j = E1[:, j * 4:(j + 1) * 4]
            E2j = E2[:, j * 4:(j + 1) * 4]
            GSELj = GSEL[:, j * 4:(j + 1) * 4]
            NMj = NM[:, j:j + 1]
            SUj = SU[:, j:j + 1]
            RVj = RV[:, j:j + 1]
            G1j = G1[:, j:j + 1]
            T1j = T1[:, j:j + 1]
            bank = 2 + j % 2

            @blk.tensor
            def _(te, j=j, bank=bank, lg_v=lg_v):
                wt(te, lg_v)
                ctx.bankwait(te, bank)
                ctx.mm.inc(te.transpose(ctx.psum[bank][:, 0:8],
                                        LG[0:8, j * 128:(j + 1) * 128],
                                        ctx.idn[0:8, 0:8]))
            mmv = ctx.mm.now()

            @blk.vector
            def _(v, LTj=LTj, NMj=NMj, bank=bank, mmv=mmv):
                wt(v, mmv)
                v.tensor_copy(LTj, ctx.psum[bank][:, 0:8])
                v.drain()
                v.tensor_reduce(NMj, LTj, AX.X, ALU.max, negate=True)
                ctx.ev.inc(v.drain())
            ctx.bankw[bank] = ctx.ev.now()
            nm_v = ctx.ev.now()

            @blk.scalar
            def _(s, LTj=LTj, ETj=ETj, NMj=NMj, SUj=SUj, nm_v=nm_v):
                wt(s, nm_v)
                ctx.sc.inc(s.activation(ETj, LTj, AF.Exp, bias=NMj,
                                        accum_out=SUj))
            ex_v = ctx.sc.now()

            @blk.vector
            def _(v, j=j, ETj=ETj, M8j=M8j, W1j=W1j, M2j=M2j, GSj=GSj,
                  GS2j=GS2j, E1j=E1j, E2j=E2j, GSELj=GSELj, SUj=SUj,
                  RVj=RVj, G1j=G1j, T1j=T1j, ex_v=ex_v):
                DR = v.drain
                wt(v, ex_v)
                v.reciprocal(RVj, SUj)
                DR()
                v.tensor_single_scalar(ETj, ETj, RVj, ALU.mult)
                DR()
                v.tensor_tensor(GSj, ETj[:, 0:8:2], ETj[:, 1:8:2], ALU.max)
                DR()
                v.tensor_reduce(G1j, GSj, AX.X, ALU.max)
                DR()
                v.tensor_single_scalar(E1j, GSj, G1j, ALU.is_ge)
                DR()
                v.tensor_mul(E1j, E1j, GSj)
                DR()
                v.tensor_sub(GS2j, GSj, E1j)
                DR()
                v.tensor_reduce(G1j, GS2j, AX.X, ALU.max)
                DR()
                v.tensor_single_scalar(E2j, GS2j, G1j, ALU.is_ge)
                v.tensor_single_scalar(GSELj, E1j, 0.0, ALU.is_gt)
                DR()
                v.tensor_add(GSELj, GSELj, E2j)
                DR()
                v.tensor_mul(M8j[:, 0:8:2], ETj[:, 0:8:2], GSELj)
                v.tensor_mul(M8j[:, 1:8:2], ETj[:, 1:8:2], GSELj)
                DR()
                v.tensor_reduce(T1j, M8j, AX.X, ALU.max)
                DR()
                v.tensor_single_scalar(W1j, M8j, T1j, ALU.is_ge)
                DR()
                v.tensor_mul(W1j, W1j, M8j)
                DR()
                v.tensor_sub(M2j, M8j, W1j)
                DR()
                v.tensor_reduce(T1j, M2j, AX.X, ALU.max)
                DR()
                v.tensor_single_scalar(ETj, M2j, T1j, ALU.is_ge)
                DR()
                v.tensor_mul(M2j, M2j, ETj)
                DR()
                v.tensor_add(W1j, W1j, M2j)
                ctx.ev.inc(DR())

        prev_wb = None
        for n in range(NE):
            src = W1[:, n:64:8]
            dst = ctx.rF[:, 0:1024]
            bcast_cols(ctx, [(src, dst)], 8, extra_wait=prev_wb)
            bc_v = ctx.ev.now()

            @blk.scalar
            def _(s, n=n, tb=tb, dst=dst, bc_v=bc_v):
                wt(s, bc_v)
                ctx.store(s, WB[n, :, tb * TB:(tb + 1) * TB], dst, so)
            prev_wb = so.now()
    ctx.barrier()


def moe_hidden(ctx, HDN, GU, FF, wb_hbm=None):
    blk = ctx.blk
    lc = ctx.dsem("lc")
    wb_ld = None
    if wb_hbm is not None:
        @blk.sync
        def _(q):
            ctx.load(q, ctx.rS[:, 0:2048], wb_hbm, lc)
        wb_ld = lc.now()

    gr = DRing(ctx, ctx.rE, 2, 2048, loads="la")
    ur = DRing(ctx, ctx.rF, 2, 2048, loads="lb", stores="se")
    for fc in range(FF // 128):
        gsl = gr.take()
        usl = ur.take()

        @blk.sync
        def _(q, fc=fc, gsl=gsl, usl=usl):
            wt(q, gsl.reuse, usl.reuse)
            ctx.load(q, gsl.ap, GU[fc * 128:(fc + 1) * 128, :], gsl.lsem)
            ctx.load(q, usl.ap, GU[FF + fc * 128: FF + (fc + 1) * 128, :],
                     usl.lsem)
        lg, lu = gsl.lsem.now(), usl.lsem.now()

        @blk.scalar
        def _(s, gsl=gsl):
            wt(s, lg)
            ctx.sc.inc(s.activation(gsl.ap, gsl.ap, AF.Silu))
        scv = ctx.sc.now()

        @blk.vector
        def _(v, gsl=gsl, usl=usl):
            wt(v, scv, lu, wb_ld)
            if wb_hbm is not None:
                v.tensor_mul(usl.ap, gsl.ap, usl.ap)
                v.drain()
                ctx.ev.inc(v.tensor_mul(usl.ap, usl.ap, ctx.rS[:, 0:2048]))
            else:
                ctx.ev.inc(v.tensor_mul(usl.ap, gsl.ap, usl.ap))
        evv = ctx.ev.now()

        @blk.scalar
        def _(s, fc=fc, usl=usl, evv=evv):
            wt(s, evv)
            ctx.store(s, HDN[fc * 128:(fc + 1) * 128, :], usl.ap, usl.ssem)
        stv = usl.ssem.now()
        gsl.release(stv)
        usl.release(stv)
    ctx.barrier()


def attention(ctx, YA, QKV, VT, masks_hbm, *, nh, nkv, hd, causal):
    blk = ctx.blk
    A = ctx.rA
    kh = A[:, 0:2048]
    qh = A[:, 2048:4096]
    vt = A[:, 4096:6144]
    yb = A[:, 6144:8192]
    S4 = [A[:, 8192 + i * 2048: 8192 + (i + 1) * 2048] for i in range(4)]

    def PTc(c):
        r = ctx.rE if c < 8 else ctx.rF
        return r[:, (c % 8) * 512:(c % 8) * 512 + 512]

    g = ctx.gsmall
    nm = [g[:, i:i + 1] for i in range(4)]
    su = [g[:, 4 + i:5 + i] for i in range(4)]
    rv = [g[:, 8 + i:9 + i] for i in range(4)]
    MSK = ctx.rC[:, 0:2048]
    lc = ctx.dsem("lc")
    lk = ctx.dsem("lw0")
    lq = ctx.dsem("lw1")
    so = ctx.dsem("so1")

    if causal:
        @blk.sync
        def _(q):
            for a in range(4):
                ctx.load(q, MSK[:, a * 512:(a + 1) * 512], masks_hbm[a], lc)
    msk_ld = lc.now()
    rep = nh // nkv
    y_store = None

    for kv in range(nkv):
        @blk.sync
        def _(q, kv=kv):
            wt(q, ctx.mm.now())
            ctx.load(q, kh[0:hd, :],
                     QKV[2048 + kv * hd: 2048 + (kv + 1) * hd, :], lk)
            for c in range(16):
                ctx.load(q, vt[:, c * hd:(c + 1) * hd],
                         VT[c * 128:(c + 1) * 128, kv * hd:(kv + 1) * hd], lk)
        kv_ld = lk.now()

        for qi in range(rep):
            h = kv * rep + qi

            @blk.sync
            def _(q, h=h):
                wt(q, ctx.mm.now())
                ctx.load(q, qh[0:hd, :], QKV[h * hd:(h + 1) * hd, :], lq)
            q_ld = lq.now()

            for tqg in range(4):
                EXT = (tqg + 1) * 512 if causal else 2048
                NKC = EXT // 128
                exps = []
                for i in range(4):
                    tqt = tqg * 4 + i
                    S = S4[i]
                    last_ev = last_sc = None
                    for tkb in range(EXT // 512):
                        bank = (i * (EXT // 512) + tkb) % 2

                        @blk.tensor
                        def _(te, tqt=tqt, tkb=tkb, bank=bank):
                            wt(te, kv_ld, q_ld, msk_ld)
                            ctx.bankwait(te, bank)
                            ctx.mm.inc(te.matmul(
                                ctx.psum[bank],
                                qh[0:hd, tqt * 128:(tqt + 1) * 128],
                                kh[0:hd, tkb * 512:(tkb + 1) * 512],
                                start=True, stop=True))
                        mmv = ctx.mm.now()

                        diag = causal and (tkb == tqg)
                        if diag:
                            @blk.vector
                            def _(v, i=i, tqt=tqt, tkb=tkb, bank=bank, mmv=mmv, S=S):
                                wt(v, mmv)
                                ctx.ev.inc(v.tensor_add(
                                    S[:, tkb * 512:(tkb + 1) * 512],
                                    ctx.psum[bank],
                                    MSK[:, (tqt % 4) * 512:(tqt % 4 + 1) * 512]))
                            ctx.bankw[bank] = ctx.ev.now()
                            last_ev = ctx.ev.now()
                        else:
                            @blk.scalar
                            def _(s, tkb=tkb, bank=bank, mmv=mmv, S=S):
                                wt(s, mmv)
                                ctx.sc.inc(s.copy(
                                    S[:, tkb * 512:(tkb + 1) * 512],
                                    ctx.psum[bank]))
                            ctx.bankw[bank] = ctx.sc.now()
                            last_sc = ctx.sc.now()

                    @blk.vector
                    def _(v, i=i, S=S, EXT=EXT, last_ev=last_ev, last_sc=last_sc):
                        wt(v, last_ev, last_sc)
                        ctx.ev.inc(v.tensor_reduce(nm[i], S[:, 0:EXT], AX.X,
                                                   ALU.max, negate=True))
                    exps.append(ctx.ev.now())

                ex_vs = []
                for i in range(4):
                    @blk.scalar
                    def _(s, i=i, EXT=EXT):
                        wt(s, (ctx.ev, exps[i][1]))
                        S = S4[i]
                        ctx.sc.inc(s.activation(S[:, 0:EXT], S[:, 0:EXT],
                                                AF.Exp, bias=nm[i],
                                                accum_out=su[i]))
                    ex_vs.append(ctx.sc.now())

                @blk.vector
                def _(v, EXT=EXT, ex_vs=ex_vs):
                    wt(v, ex_vs[3])
                    for i in range(4):
                        v.reciprocal(rv[i], su[i])
                    v.drain()
                    for i in range(4):
                        v.tensor_single_scalar(S4[i][:, 0:EXT], S4[i][:, 0:EXT],
                                               rv[i], ALU.mult)
                    ctx.ev.inc(v.drain())
                exps = [ctx.ev.now()] * 4

                pt_evs = None
                for c in range(NKC):
                    bank = 2 + c % 2

                    @blk.tensor
                    def _(te, c=c, bank=bank, exps=exps):
                        wt(te, exps[3])
                        ctx.bankwait(te, bank)
                        ins = None
                        for i in range(4):
                            ins = te.transpose(
                                ctx.psum[bank][:, i * 128:(i + 1) * 128],
                                S4[i][:, c * 128:(c + 1) * 128], ctx.idn)
                        ctx.mm.inc(ins)
                    mv = ctx.mm.now()

                    @blk.vector
                    def _(v, c=c, bank=bank, mv=mv):
                        wt(v, mv)
                        ctx.ev.inc(v.tensor_copy(PTc(c), ctx.psum[bank]))
                    ctx.bankw[bank] = ctx.ev.now()
                    pt_evs = ctx.ev.now()

                @blk.tensor
                def _(te, EXT=EXT, NKC=NKC, pt_evs=pt_evs):
                    wt(te, pt_evs)
                    ctx.bankwait(te, 4)
                    ins = None
                    for c in range(NKC):
                        ins = te.matmul(ctx.psum[4][0:hd, :],
                                        vt[:, c * hd:(c + 1) * hd], PTc(c),
                                        start=(c == 0), stop=(c == NKC - 1))
                    ctx.mm.inc(ins)
                y_mm = ctx.mm.now()

                @blk.scalar
                def _(s, tqg=tqg, y_mm=y_mm, y_store=y_store):
                    wt(s, y_mm)
                    if tqg == 0:
                        wt(s, y_store)
                    ctx.sc.inc(s.copy(yb[0:hd, tqg * 512:(tqg + 1) * 512],
                                      ctx.psum[4][0:hd, :]))
                ctx.bankw[4] = ctx.sc.now()

            y_done = ctx.sc.now()

            @blk.scalar
            def _(s, h=h, y_done=y_done):
                wt(s, y_done)
                ctx.store(s, YA[h * hd:(h + 1) * hd, :], yb[0:hd, :], so)
            y_store = so.now()
    ctx.barrier()


def weight_quant(ctx, WQ, wT, K, M, INVW):
    blk = ctx.blk
    ntk = K // 128
    ntm = M // 2048
    NTT = ntk * ntm
    acc = ctx.rC[:, 0:NTT]
    xr = DRing(ctx, ctx.rE, 2, 2048, loads="la")
    so = ctx.dsem("so2")

    for i in range(NTT):
        ck, cm = i // ntm, i % ntm
        xsl = xr.take()

        @blk.sync
        def _(q, ck=ck, cm=cm, xsl=xsl):
            wt(q, xsl.reuse)
            ctx.load(q, xsl.ap, wT[ck * 128:(ck + 1) * 128,
                                   cm * 2048:(cm + 1) * 2048], xsl.lsem)
        ldv = xsl.lsem.now()

        @blk.vector
        def _(v, i=i, xsl=xsl, ldv=ldv):
            wt(v, ldv)
            ctx.ev.inc(v.tensor_reduce(acc[:, i:i + 1], xsl.ap, AX.X, ALU.add,
                                       apply_absolute_value=True))
        xsl.release(ctx.ev.now())

    SW = ctx.gsmall[:, 0:1]
    IW = ctx.gsmall[:, 1:2]
    t11 = ctx.gsmall[:, 2:3]

    @blk.vector
    def _(v):
        wt(v, ctx.ev.now())
        v.tensor_reduce(t11, acc[:, 0:NTT], AX.X, ALU.add)
        ctx.ev.inc(v.drain())
    rs_v = ctx.ev.now()

    @blk.tensor
    def _(te, rs_v=rs_v):
        wt(te, rs_v)
        ctx.bankwait(te, 6)
        ctx.mm.inc(te.transpose(ctx.psum[6][0:1, 0:128], t11, ctx.idn))
    tr_v = ctx.mm.now()

    @blk.vector
    def _(v, tr_v=tr_v):
        wt(v, tr_v)
        row = ctx.trow[0:1, 0:128]
        v.tensor_copy(row, ctx.psum[6][0:1, 0:128])
        v.drain()
        v.tensor_reduce(ctx.trow[0:1, 128:129], row, AX.X, ALU.add)
        v.drain()
        v.tensor_scalar(ctx.trow[0:1, 129:130], ctx.trow[0:1, 128:129],
                        1.0 / (K * M), 1e-5, ALU.mult, ALU.max)
        v.drain()
        v.reciprocal(ctx.trow[0:1, 130:131], ctx.trow[0:1, 129:130])
        ctx.ev.inc(v.drain())
    ctx.bankw[6] = ctx.ev.now()
    sw_v = ctx.ev.now()

    for src_off, dst in ((130, SW), (129, IW)):
        @blk.tensor
        def _(te, src_off=src_off, sw_v=sw_v):
            wt(te, sw_v)
            ctx.bankwait(te, 7)
            ctx.mm.inc(te.matmul(ctx.psum[7][:, 0:1], ctx.ones_row,
                                 ctx.trow[0:1, src_off:src_off + 1],
                                 start=True, stop=True))
        bv = ctx.mm.now()

        @blk.vector
        def _(v, dst=dst, bv=bv):
            wt(v, bv)
            ctx.ev.inc(v.tensor_copy(dst, ctx.psum[7][:, 0:1]))
        ctx.bankw[7] = ctx.ev.now()
    swb_v = ctx.ev.now()

    @blk.scalar
    def _(s, swb_v=swb_v):
        wt(s, swb_v)
        ctx.store(s, INVW, IW, so)

    br = DRing(ctx, ctx.rB.bitcast(BF16), 2, 2048, stores="se")
    scr = ctx.rF[:, 0:2048]
    scr2 = ctx.rF[:, 2048:4096]
    for i in range(NTT):
        ck, cm = i // ntm, i % ntm
        xsl = xr.take()
        bsl = br.take()

        @blk.sync
        def _(q, ck=ck, cm=cm, xsl=xsl):
            wt(q, xsl.reuse)
            ctx.load(q, xsl.ap, wT[ck * 128:(ck + 1) * 128,
                                   cm * 2048:(cm + 1) * 2048], xsl.lsem)
        ldv = xsl.lsem.now()

        @blk.vector
        def _(v, xsl=xsl, bsl=bsl, ldv=ldv, swb_v=swb_v):
            wt(v, ldv, bsl.reuse, swb_v)
            v.tensor_single_scalar(xsl.ap, xsl.ap, SW, ALU.mult)
            emit_round(v, xsl.ap, scr, scr2, ctx.rR)
            v.tensor_scalar(xsl.ap, xsl.ap, -1.0, 1.0, ALU.max, ALU.min)
            v.drain()
            ctx.ev.inc(v.tensor_copy(bsl.ap, xsl.ap))
        evv = ctx.ev.now()

        @blk.scalar
        def _(s, ck=ck, cm=cm, bsl=bsl, evv=evv):
            wt(s, evv)
            ctx.store(s, WQ[ck * 128:(ck + 1) * 128, cm * 2048:(cm + 1) * 2048],
                      bsl.ap, bsl.ssem)
        stv = bsl.ssem.now()
        xsl.release(stv)
        bsl.release(stv)
    ctx.barrier()


def actquant(ctx, XQ, src, nchunks, SCL_hbm, INVW, extra, stats_hbm=None):
    blk = ctx.blk
    am = ctx.rS[:, 2048:4096]
    xr = DRing(ctx, ctx.rE, 2, 2048, loads="la")
    lc = ctx.dsem("lc")
    so = ctx.dsem("so3")

    if stats_hbm is None:
        @blk.vector
        def _(v):
            ctx.ev.inc(v.memset(am, 0.0))
        for c in range(nchunks):
            xsl = xr.take()

            @blk.sync
            def _(q, c=c, xsl=xsl):
                wt(q, xsl.reuse)
                ctx.load(q, xsl.ap, src[c * 128:(c + 1) * 128, :], xsl.lsem)
            ldv = xsl.lsem.now()

            @blk.vector
            def _(v, xsl=xsl, ldv=ldv):
                wt(v, ldv)
                v.drain()
                v.tensor_tensor(am, am, xsl.ap, ALU.max)
                v.drain()
                ctx.ev.inc(v.scalar_tensor_tensor(am, xsl.ap, -1.0, am,
                                                  ALU.mult, ALU.max))
            xsl.release(ctx.ev.now())
    else:
        @blk.sync
        def _(q):
            ctx.load(q, am, stats_hbm, lc)
        am_ld = lc.now()

        @blk.vector
        def _(v):
            wt(v, am_ld)
            ctx.ev.inc(v.tensor_copy(am, am))
    am_v = ctx.ev.now()

    AMAX = ctx.stats[:, 96:112]
    stg = ctx.sqst[:, 0:128]
    for j in range(16):
        bank = j % 2

        @blk.tensor
        def _(te, j=j, bank=bank, am_v=am_v):
            wt(te, am_v)
            ctx.bankwait(te, bank)
            ctx.mm.inc(te.transpose(ctx.psum[bank][:, 0:128],
                                    am[:, j * 128:(j + 1) * 128], ctx.idn))
        mv = ctx.mm.now()

        @blk.vector
        def _(v, j=j, bank=bank, mv=mv):
            wt(v, mv)
            v.drain()
            v.tensor_copy(stg, ctx.psum[bank][:, 0:128])
            v.drain()
            ctx.ev.inc(v.tensor_reduce(AMAX[:, j:j + 1], stg, AX.X, ALU.max))
        ctx.bankw[bank] = ctx.ev.now()

    SC16 = ctx.stats[:, 112:128]
    IV16 = ctx.stats[:, 128:144]
    iw = ctx.gsmall[:, 8:9]
    lc2 = ctx.dsem("lc2")

    @blk.sync
    def _(q):
        ctx.load(q, iw, INVW, lc2)
    iw_ld = lc2.now()

    @blk.vector
    def _(v):
        wt(v, ctx.ev.now())
        wt(v, iw_ld)
        v.tensor_scalar_max(AMAX, AMAX, 1e-5)
        v.drain()
        v.reciprocal(SC16, AMAX)
        v.tensor_scalar_mul(IV16, AMAX, 1.0 / 127.0)
        v.drain()
        ctx.ev.inc(v.tensor_scalar_mul(SC16, SC16, 127.0))

    SCb = ctx.rS[:, 0:2048]
    IVb = am
    bcast_cols(ctx, [(SC16, SCb), (IV16, IVb)], 16)

    @blk.vector
    def _(v):
        wt(v, ctx.ev.now())
        v.tensor_single_scalar(IVb, IVb, iw, ALU.mult)
        if extra != 1.0:
            v.drain()
            v.tensor_scalar_mul(IVb, IVb, extra)
        ctx.ev.inc(v.drain())
    scl_v = ctx.ev.now()

    @blk.scalar
    def _(s, scl_v=scl_v):
        wt(s, scl_v)
        ctx.store(s, SCL_hbm, IVb, so)

    br = DRing(ctx, ctx.rB.bitcast(BF16), 2, 2048, stores="se")
    scr = ctx.rF[:, 0:2048]
    scr2 = ctx.rF[:, 2048:4096]
    for c in range(nchunks):
        xsl = xr.take()
        bsl = br.take()

        @blk.sync
        def _(q, c=c, xsl=xsl):
            wt(q, xsl.reuse)
            ctx.load(q, xsl.ap, src[c * 128:(c + 1) * 128, :], xsl.lsem)
        ldv = xsl.lsem.now()

        @blk.vector
        def _(v, xsl=xsl, bsl=bsl, ldv=ldv, scl_v=scl_v):
            wt(v, ldv, bsl.reuse, scl_v)
            v.tensor_mul(xsl.ap, xsl.ap, SCb)
            emit_round(v, xsl.ap, scr, scr2, ctx.rR)
            v.tensor_scalar(xsl.ap, xsl.ap, -128.0, 127.0, ALU.max, ALU.min)
            v.drain()
            ctx.ev.inc(v.tensor_copy(bsl.ap, xsl.ap))
        evv = ctx.ev.now()

        @blk.scalar
        def _(s, c=c, bsl=bsl, evv=evv):
            wt(s, evv)
            ctx.store(s, XQ[c * 128:(c + 1) * 128, :], bsl.ap, bsl.ssem)
        stv = bsl.ssem.now()
        xsl.release(stv)
        bsl.release(stv)
    ctx.barrier()


def build_program(n_stages=None, dump=()):
    _WCACHE.clear()
    nc = bass.Bass("TRN2", target_bir_lowering=False)

    def din(name, shape, dt=F32):
        return nc.dram_tensor(name, list(shape), dt, kind="ExternalInput").ap()

    def dsc(name, shape, dt=F32):
        return nc.dram_tensor(name, list(shape), dt, kind="Internal").ap()

    x_in = din("x", [D, T])
    out = nc.dram_tensor("out", [D, T], F32, kind="ExternalOutput").ap()

    W = {}
    for name, shape in [
        ("gwT", [D, NE]), ("guT", [NE, 11, 128, 4096]), ("dnT", [NE, 8, 128, 2816]),
        ("sguT", [22, 128, 4096]), ("sdT", [16, 128, 2816]),
        ("qkv0T", [12, 128, 4096]), ("o0T", [8, 128, 4096]),
        ("qkv1T", [12, 128, 4096]), ("o1T", [8, 128, 4096]),
        ("w1T", [D, FH]), ("w2T", [FH, D]), ("gammaR", [128, 16]),
        ("R0C", [128, T]), ("R0S", [128, T]), ("R1C", [128, T]), ("R1S", [128, T]),
        ("PSW0", [128, 128]), ("PSW1", [128, 128]), ("MASKS", [4, 128, 512]),
        ("IDN", [128, 128]),
    ]:
        W[name] = din(name, shape)

    XR = dsc("XR", [D, T])
    LN = dsc("LNs", [D, T])
    RT = dsc("RT", [D, T])
    GU = dsc("GU", [2 * FS, T])
    HDN = dsc("HDN", [FS, T])
    WB = dsc("WBs", [NE, 128, T])
    QKV = dsc("QKV", [3072, T])
    VT = dsc("VTs", [T, 512])
    YA = dsc("YAs", [D, T])
    RMS = dsc("RMS", [D, T])
    H = dsc("Hs", [FH, T])
    XQ1 = dsc("XQ1", [D, T], BF16)
    XQ2 = dsc("XQ2", [FH, T], BF16)
    W1Q = dsc("W1Q", [D, FH], BF16)
    W2Q = dsc("W2Q", [FH, D], BF16)
    SCLA = dsc("SCLA", [128, T])
    SCLB = dsc("SCLB", [128, T])
    AM2 = dsc("AM2", [128, T])
    INVW1 = dsc("INVW1", [128, 1])
    INVW2 = dsc("INVW2", [128, 1])

    scr = dict(XR=XR, LN=LN, RT=RT, GU=GU, HDN=HDN, WB=WB, QKV=QKV, VT=VT,
               YA=YA, RMS=RMS, H=H, XQ1=XQ1, XQ2=XQ2, W1Q=W1Q, W2Q=W2Q,
               SCLA=SCLA, SCLB=SCLB, AM2=AM2, INVW1=INVW1, INVW2=INVW2)

    import contextlib
    with contextlib.ExitStack() as stack:
        blk = None

        class _C(Ctx):
            def semalloc(self, name):
                return stack.enter_context(nc.semaphore(name))

        h_mm = stack.enter_context(nc.semaphore("mm"))
        h_ev = stack.enter_context(nc.semaphore("ev"))
        h_sc = stack.enter_context(nc.semaphore("sc"))
        blk = stack.enter_context(nc.Block())
        ctx = _C(nc, blk)
        ctx.mm, ctx.ev, ctx.sc = Sem(h_mm), Sem(h_ev), Sem(h_sc)

        A = nc.alloc_sbuf_tensor
        ctx.rA = A("rA", [128, 16384], F32).ap()
        ctx.rB = A("rB", [128, 2048], F32).ap()
        ctx.rW0 = A("rW0", [128, 4096], F32).ap()
        ctx.rW1 = A("rW1", [128, 4096], F32).ap()
        ctx.rE = A("rE", [128, 4096], F32).ap()
        ctx.rF = A("rF", [128, 4096], F32).ap()
        ctx.rS = A("rS", [128, 4096], F32).ap()
        ctx.rC = A("rC", [128, 4096], F32).ap()
        ctx.rR = A("rR", [128, 2048], I32).ap()
        ctx.idn = A("idn", [128, 128], F32).ap()
        ctx.pswap = A("pswap", [128, 128], F32).ap()
        ctx.trow = A("trow", [128, 256], F32).ap()
        ctx.stats = A("stats", [128, 256], F32).ap()
        ctx.sqst = A("sqst", [128, 256], F32).ap()
        ctx.gw_sb = A("gw_sb", [128, 128], F32).ap()
        ctx.ones_col = A("ones_col", [128, 1], F32).ap()
        ctx.ones_row = A("ones_row", [1, 128], F32).ap()
        ctx.gsmall = A("gsmall", [128, 640], F32).ap()
        ctx.gammaR = A("gammaR_sb", [128, 16], F32).ap()
        for i in range(8):
            ctx.psum.append(nc.alloc_psum_tensor(f"p{i}", [128, 512], F32).ap())

        lc = ctx.dsem("lc")

        @blk.sync
        def _(q):
            ctx.load(q, ctx.idn, W["IDN"], lc)
            ctx.load(q, ctx.gammaR, W["gammaR"], lc)
        ctx.gamma_ld = lc.now()

        @blk.vector
        def _(v):
            v.memset(ctx.ones_col, 1.0)
            ctx.ev.inc(v.memset(ctx.ones_row, 1.0))

        @blk.tensor
        def _(te):
            wt(te, ctx.gamma_ld)
            wt(te, ctx.ev.now())

        stages = []

        def stage(fn, *a, **k):
            stages.append((fn, a, k))

        def moe_block(src):
            stage(norm_fm, ctx, LN, src, kind="ln")
            stage(router, ctx, WB, W["gwT"], LN)
            for n in range(NE):
                stage(gemm, ctx, GU, W["guT"][n], LN, D, 2 * FE, T)
                stage(gemm, ctx, RT, W["dnT"][n], None, FE, D, T,
                      addf=(RT if n > 0 else None),
                      rhs_fuse=(GU, FE, WB[n]))
            stage(gemm, ctx, GU, W["sguT"], LN, D, 2 * FS, T)
            stage(gemm, ctx, RT, W["sdT"], None, FS, D, T, addf=RT,
                  t_block=512, rhs_fuse=(GU, FS, None))
            stage(resadd, ctx, XR, src, RT, 0.5)

        def attn_block(layer):
            qkvT = W["qkv0T"] if layer == 0 else W["qkv1T"]
            oT = W["o0T"] if layer == 0 else W["o1T"]
            RCc = W["R0C"] if layer == 0 else W["R1C"]
            RSs = W["R0S"] if layer == 0 else W["R1S"]
            PSW = W["PSW0"] if layer == 0 else W["PSW1"]
            sm = SM0 if layer == 0 else SM1
            nh, nkv, hd = (NH0, KV0, HD0) if layer == 0 else (NH1, KV1, HD1)
            stage(norm_fm, ctx, LN, XR, kind="ln")
            stage(gemm, ctx, QKV, qkvT, LN, D, 3072, T, scale_mtiles=(16, sm))
            stage(rope, ctx, QKV, 16, RCc, RSs, PSW, row0=0)
            stage(rope, ctx, QKV, 4, RCc, RSs, PSW, row0=2048)
            stage(transpose_fm, ctx, VT, QKV[2560:3072, :], 512, T)
            stage(attention, ctx, YA, QKV, VT, W["MASKS"],
                  nh=nh, nkv=nkv, hd=hd, causal=(layer == 0))
            stage(gemm, ctx, XR, oT, YA, D, D, T, addf=XR)

        def bitffn_block(first, last):
            stage(norm_fm, ctx, LN, XR, kind="ln")
            stage(norm_fm, ctx, RMS, LN, kind="rms", eps=1e-6, gamma=ctx.gammaR)
            if first:
                stage(weight_quant, ctx, W1Q, W["w1T"], D, FH, INVW1)
                stage(weight_quant, ctx, W2Q, W["w2T"], FH, D, INVW2)
            stage(actquant, ctx, XQ1, RMS, 16, SCLA, INVW1, 1.0)
            stage(gemm, ctx, H, W1Q, XQ1, D, FH, T, dt=BF16, mul_hbm=SCLA,
                  act=AF.Gelu, stats=ctx.rS[:, 2048:4096], stats_hbm=AM2,
                  t_block=2048)
            stage(actquant, ctx, XQ2, H, 64, SCLB, INVW2, 0.5, stats_hbm=AM2)
            stage(gemm, ctx, (out if last else XR), W2Q, XQ2, FH, D, T,
                  dt=BF16, mul_hbm=SCLB, addf=XR, t_block=512)

        moe_block(x_in)
        attn_block(0)
        moe_block(XR)
        bitffn_block(first=True, last=False)
        attn_block(1)
        bitffn_block(first=False, last=True)

        for i, (fn, a, k) in enumerate(stages):
            if n_stages is not None and i >= n_stages:
                break
            ctx.begin()
            fn(*a, **k)

        if dump:
            for name in dump:
                ap = scr[name]
                dbg = nc.dram_tensor("dbg_" + name, list(ap.shape), ap.dtype,
                                     kind="ExternalOutput").ap()

                @blk.sync
                def _(q, ap=ap, dbg=dbg):
                    wt(q, ctx.barwaits)
                    ctx.store(q, dbg, ap, ctx.dsem("so0"))

        final = list(ctx.barwaits) + [(s, n) for s, n in ctx.dirty.items()]

        @blk.sync
        def _(q):
            wt(q, final)
    return nc


def stage_inputs(inputs):
    g = {k: np.asarray(v) for k, v in inputs.items()}
    f = np.float32
    ct = np.ascontiguousarray

    def TT(a):
        return ct(np.asarray(a).T)

    m = {}
    m["gwT"] = TT(g["L0_ffn_gate_weight"])
    m["guT"] = ct(np.concatenate([
        np.transpose(g["L0_ffn_gate_w"], (0, 2, 1)),
        np.transpose(g["L0_ffn_up_w"], (0, 2, 1))], axis=2))
    m["dnT"] = ct(np.transpose(g["L0_ffn_down_w"], (0, 2, 1)))
    m["sguT"] = ct(np.concatenate([TT(g["L0_ffn_sh_gate_w"]),
                                   TT(g["L0_ffn_sh_up_w"])], axis=1))
    m["sdT"] = TT(g["L0_ffn_sh_down_w"])
    m["qkv0T"] = ct(np.concatenate([TT(g["L0_attn_wq"]), TT(g["L0_attn_wk"]),
                                    TT(g["L0_attn_wv"])], axis=1))
    m["o0T"] = TT(g["L0_attn_wo"])
    m["qkv1T"] = ct(np.concatenate([TT(g["L1_attn_Wq"]), TT(g["L1_attn_Wk"]),
                                    TT(g["L1_attn_Wv"])], axis=1))
    m["o1T"] = TT(g["L1_attn_Wo"])
    m["w1T"] = TT(g["L1_ffn_w1"])
    m["w2T"] = TT(g["L1_ffn_w2"])
    m["gammaR"] = ct(np.asarray(g["L1_ffn_gamma"], f).reshape(16, 128).T)

    def group_w(w, esz=4):
        K, M = w.shape
        KC = K // 128
        wcap = 4096 * 4 // esz
        GM = 2 if KC * 256 <= wcap else 1
        gw = GM * 128
        NG = M // gw
        blocks = []
        for gg in range(NG):
            b = w[:, gg * gw:(gg + 1) * gw].reshape(KC, 128, gw)
            blocks.append(np.transpose(b, (1, 0, 2)).reshape(128, KC * gw))
        return ct(np.stack(blocks, axis=0))

    for k in ("qkv0T", "o0T", "qkv1T", "o1T", "sguT", "sdT"):
        m[k] = group_w(m[k])
    m["guT"] = ct(np.stack([group_w(m["guT"][n]) for n in range(NE)], axis=0))
    m["dnT"] = ct(np.stack([group_w(m["dnT"][n]) for n in range(NE)], axis=0))

    t = np.arange(T, dtype=np.float64)
    d = np.arange(128)
    th0 = 1.0 / (10000.0 ** (np.arange(0, HD0, 2, dtype=np.float64) / HD0))
    fr0 = th0[:, None] * t[None, :]
    i0 = (d % 64) // 2
    sgn0 = np.where(d % 2 == 0, -1.0, 1.0)
    m["R0C"] = ct(np.cos(fr0)[i0, :].astype(f))
    m["R0S"] = ct((sgn0[:, None] * np.sin(fr0)[i0, :]).astype(f))
    th1 = 1.0 / (10000.0 ** (np.arange(0, HD1, 2, dtype=np.float64) / HD1))
    fr1 = th1[:, None] * t[None, :]
    i1 = d % 64
    sgn1 = np.where(d < 64, -1.0, 1.0)
    m["R1C"] = ct(np.cos(fr1)[i1, :].astype(f))
    m["R1S"] = ct((sgn1[:, None] * np.sin(fr1)[i1, :]).astype(f))

    p0 = np.zeros((128, 128), f)
    p0[np.arange(128), np.arange(128) ^ 1] = 1.0
    m["PSW0"] = p0
    p1 = np.zeros((128, 128), f)
    p1[np.arange(128), np.arange(128) ^ 64] = 1.0
    m["PSW1"] = p1

    masks = np.zeros((4, 128, 512), f)
    p = np.arange(128)[:, None]
    c = np.arange(512)[None, :]
    for a_ in range(4):
        masks[a_] = np.where(c <= 128 * a_ + p, 0.0, NEGBIG)
    m["MASKS"] = masks
    m["IDN"] = np.eye(128, dtype=f)
    for k in m:
        assert m[k].dtype == np.float32, (k, m[k].dtype)
    return m


def kernel(**inputs):
    m = stage_inputs(inputs)
    x = np.asarray(inputs["L0_attn_x"])
    nc = build_program(n_stages=N_STAGES, dump=tuple(DUMP))
    in_maps = []
    for b in range(8):
        d = dict(m)
        d["x"] = np.ascontiguousarray(x[b].T)
        in_maps.append(d)
    res = run_bass_kernel_spmd(nc, in_maps, core_ids=list(range(8)))
    outs = [np.asarray(r["out"]).T for r in res.results]
    return np.stack(outs, axis=0)


# version 10
# speedup vs baseline: 1.1329x; 1.1028x over previous
import math
import os
import numpy as np
import concourse.bass as bass
import concourse.mybir as mybir
from concourse.bass_utils import run_bass_kernel_spmd

os.environ.setdefault("GAUGE_NTFF_IGNORE_DMA", "1")

F32 = mybir.dt.float32
BF16 = mybir.dt.bfloat16
I32 = mybir.dt.int32
AX = mybir.AxisListType
ALU = mybir.AluOpType
AF = mybir.ActivationFunctionType

D = 2048
T = 2048
NE = 8
FE = 1408
FS = 2816
FH = 8192
NH0, KV0, HD0 = 32, 8, 64
NH1, KV1, HD1 = 16, 4, 128
SM0 = 1.0 / math.sqrt(HD0)
SM1 = 1.0 / math.sqrt(HD1)
NEGBIG = -1e30

N_STAGES = None
DUMP = []


class Sem:
    def __init__(self, handle):
        self.h = handle
        self.n = 0

    def inc(self, ins, by=1):
        ins.then_inc(self.h, by)
        self.n += by
        return self.n

    def now(self):
        return (self, self.n)


_WCACHE = {}


def wt(eng, *waits):
    key0 = eng.engine

    def one(w):
        k = (key0, id(w[0]))
        if _WCACHE.get(k, -1) >= w[1]:
            return
        _WCACHE[k] = w[1]
        eng.wait_ge(w[0].h, w[1])

    for w in waits:
        if w is None:
            continue
        if isinstance(w, list):
            for x in w:
                if x is not None:
                    one(x)
        else:
            one(w)


class DRing:
    def __init__(self, ctx, ap, nslots, width, loads=None, stores=None):
        self.ap = ap
        self.ns = nslots
        self.w = width
        self.ls = [ctx.dsem(f"{loads}{i}") for i in range(nslots)] if loads else None
        self.ss = [ctx.dsem(f"{stores}{i}") for i in range(nslots)] if stores else None
        self.reuse = [[] for _ in range(nslots)]
        self.i = 0

    def take(self):
        s = self.i % self.ns
        self.i += 1
        return RSlot(self, s)

    def release(self, s, *semvals):
        self.reuse[s] = [sv for sv in semvals if sv is not None]


class RSlot:
    def __init__(self, ring, s):
        self.ring = ring
        self.s = s
        self.ap = ring.ap[:, s * ring.w:(s + 1) * ring.w]
        self.lsem = ring.ls[s] if ring.ls else None
        self.ssem = ring.ss[s] if ring.ss else None
        self.reuse = list(ring.reuse[s])

    def release(self, *semvals):
        self.ring.release(self.s, *semvals)


class Ctx:
    def __init__(self, nc, blk):
        self.nc = nc
        self.blk = blk
        self.psum = []
        self.bankw = [None] * 8
        self.dsems = {}
        self.dirty = {}
        self.barwaits = []

    def dsem(self, name):
        if name not in self.dsems:
            self.dsems[name] = Sem(self.semalloc(name))
        return self.dsems[name]

    def load(self, q, out, in_, sem):
        sem.inc(q.dma_start(out=out, in_=in_), 16)

    def store(self, q, out, in_, sem):
        sem.inc(q.dma_start(out=out, in_=in_), 16)
        self.dirty[sem] = sem.n

    def barrier(self):
        self.barwaits = [(s, n) for s, n in self.dirty.items()]
        self.dirty = {}
        self.bankw = [None] * 8

    def begin(self):
        b = list(self.barwaits)
        blk = self.blk

        @blk.sync
        def _(q):
            wt(q, b)

        @blk.tensor
        def _(te):
            wt(te, b)

        @blk.vector
        def _(v):
            wt(v, b)

        @blk.scalar
        def _(s):
            wt(s, b)

    def bankwait(self, eng, bank):
        w = self.bankw[bank]
        if w is not None:
            wt(eng, w)


def emit_round_pair(v, yA, yB, sA, s2A, iA, sB, s2B, iB):
    WA, WB = yA.shape[1], yB.shape[1]
    iA = iA[:, 0:WA]
    iB = iB[:, 0:WB]
    v.tensor_copy(iA, yA)
    v.tensor_copy(iB, yB)
    v.tensor_copy(sA, iA)
    v.tensor_copy(sB, iB)
    v.tensor_sub(yA, yA, sA)
    v.tensor_sub(yB, yB, sB)
    v.tensor_single_scalar(s2A, yA, 0.5, ALU.is_ge)
    v.tensor_single_scalar(s2B, yB, 0.5, ALU.is_ge)
    v.tensor_add(sA, sA, s2A)
    v.tensor_add(sB, sB, s2B)
    v.tensor_single_scalar(yA, yA, -0.5, ALU.is_le)
    v.tensor_single_scalar(yB, yB, -0.5, ALU.is_le)
    v.tensor_sub(yA, sA, yA)
    v.tensor_sub(yB, sB, yB)


def emit_round(v, y, scr, scr2, i32t):
    W = y.shape[1]
    i32v = i32t[:, 0:W]
    v.drain()
    v.tensor_copy(i32v, y)
    v.drain()
    v.tensor_copy(scr, i32v)
    v.drain()
    v.tensor_sub(y, y, scr)
    v.drain()
    v.tensor_single_scalar(scr2, y, 0.5, ALU.is_ge)
    v.drain()
    v.tensor_add(scr, scr, scr2)
    v.tensor_single_scalar(y, y, -0.5, ALU.is_le)
    v.drain()
    v.tensor_sub(y, scr, y)
    v.drain()


def gemm(ctx, out, wT, rhs, K, M, N, *,
         dt=F32, scale=None, scale_mtiles=None, mul_hbm=None, act=None,
         addf=None, stats=None, stats_hbm=None, t_block=1024):
    blk = ctx.blk
    esz = 4 if dt == F32 else 2
    KC = K // 128
    MT = M // 128
    TB = t_block
    NB = N // TB
    NT = TB // 512
    wcap = 4096 * 4 // esz
    GM = 2 if KC * 256 <= wcap else 1
    NG = (MT + GM - 1) // GM
    assert KC * TB * esz <= 65536, "rhs slab too large"
    assert KC * GM * 128 <= wcap, "weight slab too large"

    slab = ctx.rA if dt == F32 else ctx.rA.bitcast(BF16)
    wbufs = [ctx.rW0, ctx.rW1] if dt == F32 else \
            [ctx.rW0.bitcast(BF16), ctx.rW1.bitcast(BF16)]
    lda = ctx.dsem("lda")
    lw = [ctx.dsem("lw0"), ctx.dsem("lw1")]
    er = DRing(ctx, ctx.rE, 8, 512, stores="se")
    fr = DRing(ctx, ctx.rF, 8, 512, loads="lf") if addf is not None else None
    mulsb = ctx.rS[:, 0:N] if mul_hbm is not None else None
    if mul_hbm is not None:
        @blk.sync
        def _(q):
            ctx.load(q, mulsb, mul_hbm, lda)
    if stats is not None:
        @blk.vector
        def _(v):
            ctx.ev.inc(v.memset(stats, 0.0))

    for tb in range(NB):
        mm_before = ctx.mm.n

        @blk.sync
        def _(q, tb=tb):
            if tb > 0:
                wt(q, (ctx.mm, mm_before))
            for c in range(KC):
                ctx.load(q, slab[:, c * TB:(c + 1) * TB],
                         rhs[c * 128:(c + 1) * 128, tb * TB:(tb + 1) * TB], lda)
        rhs_ld = lda.now()

        wgrp_after = {}
        for g in range(NG):
            mts = list(range(g * GM, min(g * GM + GM, MT)))
            gw = len(mts) * 128
            wb = wbufs[g % 2]
            ws = lw[g % 2]

            grouped = len(wT.shape) == 3

            @blk.sync
            def _(q, g=g, mts=mts, gw=gw, wb=wb, ws=ws, tb=tb, grouped=grouped):
                if g >= 2:
                    wt(q, (ctx.mm, wgrp_after[g - 2]))
                elif tb > 0:
                    wt(q, (ctx.mm, mm_before))
                if grouped:
                    ctx.load(q, wb[:, 0:KC * gw], wT[g], ws)
                else:
                    for c in range(KC):
                        ctx.load(q, wb[:, c * gw:(c + 1) * gw],
                                 wT[c * 128:(c + 1) * 128,
                                    mts[0] * 128:mts[0] * 128 + gw], ws)
            w_ld = ws.now()

            for mi, mt in enumerate(mts):
                for t in range(NT):
                    tg = tb * NT + t
                    bank = (mt * NT + t) % 4
                    pb = ctx.psum[bank]

                    @blk.tensor
                    def _(te, mi=mi, t=t, bank=bank, pb=pb, gw=gw, wb=wb):
                        wt(te, rhs_ld, w_ld)
                        ctx.bankwait(te, bank)
                        ins = None
                        for c in range(KC):
                            ins = te.matmul(
                                pb,
                                wb[:, c * gw + mi * 128: c * gw + (mi + 1) * 128],
                                slab[:, c * TB + t * 512: c * TB + t * 512 + 512],
                                start=(c == 0), stop=(c == KC - 1))
                        ctx.mm.inc(ins)
                    mm_v = ctx.mm.now()

                    out_ap = out[mt * 128:(mt + 1) * 128, tg * 512:(tg + 1) * 512]
                    esl = er.take()
                    et = esl.ap
                    sc_imm = scale
                    if scale_mtiles is not None and sc_imm is None:
                        sc_imm = scale_mtiles[1] if mt < scale_mtiles[0] else None

                    fsl = f_ld = None
                    if addf is not None:
                        fsl = fr.take()

                        @blk.sync
                        def _(q, fsl=fsl, mt=mt, tg=tg):
                            wt(q, fsl.reuse)
                            ctx.load(q, fsl.ap, addf[
                                mt * 128:(mt + 1) * 128,
                                tg * 512:(tg + 1) * 512], fsl.lsem)
                        f_ld = fsl.lsem.now()

                    prod = None
                    if mul_hbm is not None:
                        @blk.vector
                        def _(v, et=et, pb=pb, tg=tg, esl=esl):
                            wt(v, mm_v, esl.reuse, rhs_ld)
                            ctx.ev.inc(v.tensor_mul(
                                et, pb, mulsb[:, tg * 512:(tg + 1) * 512]))
                        ctx.bankw[bank] = ctx.ev.now()
                        prod = ctx.ev.now()

                    if act is not None:
                        @blk.scalar
                        def _(s, et=et, pb=pb, prod=prod, esl=esl, sc_imm=sc_imm):
                            if prod is not None:
                                wt(s, prod)
                                src = et
                            else:
                                wt(s, mm_v, esl.reuse)
                                src = pb
                            ctx.sc.inc(s.activation(
                                et, src, act,
                                scale=1.0 if sc_imm is None else sc_imm))
                        if prod is None:
                            ctx.bankw[bank] = ctx.sc.now()
                        prod = ctx.sc.now()
                    elif prod is None:
                        @blk.vector
                        def _(v, et=et, pb=pb, esl=esl, sc_imm=sc_imm):
                            wt(v, mm_v, esl.reuse)
                            if sc_imm is not None:
                                ctx.ev.inc(v.tensor_single_scalar(
                                    et, pb, sc_imm, ALU.mult))
                            else:
                                ctx.ev.inc(v.tensor_copy(et, pb))
                        ctx.bankw[bank] = ctx.ev.now()
                        prod = ctx.ev.now()
                    elif sc_imm is not None:
                        @blk.vector
                        def _(v, et=et, sc_imm=sc_imm):
                            v.drain()
                            ctx.ev.inc(v.tensor_scalar_mul(et, et, sc_imm))
                        prod = ctx.ev.now()

                    stat_sem = None
                    if stats is not None:
                        @blk.vector
                        def _(v, et=et, tg=tg, prod=prod):
                            wt(v, prod)
                            stt = stats[:, tg * 512:(tg + 1) * 512]
                            v.drain()
                            v.tensor_tensor(stt, stt, et, ALU.max)
                            v.drain()
                            ctx.ev.inc(v.scalar_tensor_tensor(
                                stt, et, -1.0, stt, ALU.mult, ALU.max))
                        stat_sem = ctx.ev.now()

                    if addf is not None:
                        @blk.vector
                        def _(v, et=et, fsl=fsl, prod=prod, f_ld=f_ld):
                            if prod[0] is not ctx.ev:
                                wt(v, prod)
                            else:
                                v.drain()
                            wt(v, f_ld)
                            ctx.ev.inc(v.tensor_add(et, et, fsl.ap))
                        prod = ctx.ev.now()

                    @blk.scalar
                    def _(s, et=et, out_ap=out_ap, prod=prod, esl=esl):
                        wt(s, prod)
                        ctx.store(s, out_ap, et, esl.ssem)
                    stv = esl.ssem.now()
                    esl.release(stv, stat_sem)
                    if addf is not None:
                        fsl.release(stv)
            wgrp_after[g] = ctx.mm.n

    if stats is not None and stats_hbm is not None:
        ev_fin = ctx.ev.now()
        so = ctx.dsem("so0")

        @blk.scalar
        def _(s):
            wt(s, ev_fin)
            ctx.store(s, stats_hbm, stats, so)
    ctx.barrier()


def bcast_cols(ctx, pairs, nj, extra_wait=None):
    blk = ctx.blk
    ev0 = ctx.ev.n
    for i, (src, dst) in enumerate(pairs):
        for j in range(nj):
            k = (i * nj + j) % 2
            bankT = 6 + k
            bankB = 4 + k
            tr = ctx.trow[0:1, k * 128:(k + 1) * 128]

            @blk.tensor
            def _(te, src=src, j=j, bankT=bankT):
                wt(te, (ctx.ev, ev0))
                ctx.bankwait(te, bankT)
                ctx.mm.inc(te.matmul(ctx.psum[bankT][0:1, 0:128],
                                     src[:, j:j + 1], ctx.idn,
                                     start=True, stop=True))
            mv1 = ctx.mm.now()

            @blk.vector
            def _(v, tr=tr, bankT=bankT, mv1=mv1):
                wt(v, mv1)
                ctx.ev.inc(v.tensor_copy(tr, ctx.psum[bankT][0:1, 0:128]))
            ctx.bankw[bankT] = ctx.ev.now()
            trv = ctx.ev.now()

            @blk.tensor
            def _(te, tr=tr, bankB=bankB, trv=trv):
                wt(te, trv)
                ctx.bankwait(te, bankB)
                ctx.mm.inc(te.matmul(ctx.psum[bankB][:, 0:128], ctx.ones_row,
                                     tr, start=True, stop=True))
            mv2 = ctx.mm.now()

            @blk.vector
            def _(v, j=j, dst=dst, bankB=bankB, mv2=mv2):
                wt(v, mv2, extra_wait)
                ctx.ev.inc(v.tensor_copy(dst[:, j * 128:(j + 1) * 128],
                                         ctx.psum[bankB][:, 0:128]))
            ctx.bankw[bankB] = ctx.ev.now()


def norm_fm(ctx, out, in_, *, kind="ln", eps=1e-5, gamma=None):
    blk = ctx.blk
    TB = 1024
    slab = ctx.rA
    lda = ctx.dsem("lda")
    st = ctx.stats
    S16, Q16 = st[:, 0:16], st[:, 16:32]
    NM16, RS16 = st[:, 32:48], st[:, 48:64]
    VP16, SD16 = st[:, 64:80], st[:, 80:96]
    er = DRing(ctx, ctx.rE, 4, 1024, stores="se")

    for tb in range(2):
        prev_stores = [(s, n) for s, n in ctx.dirty.items()]

        @blk.sync
        def _(q, tb=tb):
            if tb > 0:
                wt(q, prev_stores)
            for c in range(16):
                ctx.load(q, slab[:, c * TB:(c + 1) * TB],
                         in_[c * 128:(c + 1) * 128, tb * TB:(tb + 1) * TB], lda)
        ld_v = lda.now()

        sq_free = [None, None]
        for j in range(8):
            bs, bq = j % 2, 2 + (j % 2)
            for c in range(16):
                sq = ctx.sqst[:, (c % 2) * 128:(c % 2) * 128 + 128]

                @blk.vector
                def _(v, c=c, j=j, sq=sq):
                    wt(v, ld_v, sq_free[c % 2])
                    xs = slab[:, c * TB + j * 128: c * TB + j * 128 + 128]
                    ctx.ev.inc(v.tensor_mul(sq, xs, xs))
                sc_v = ctx.ev.now()

                @blk.tensor
                def _(te, c=c, j=j, bs=bs, bq=bq, sq=sq):
                    wt(te, sc_v)
                    if c == 0:
                        ctx.bankwait(te, bs)
                        ctx.bankwait(te, bq)
                    if kind == "ln":
                        te.matmul(ctx.psum[bs][:, 0:1],
                                  slab[:, c * TB + j * 128: c * TB + j * 128 + 128],
                                  ctx.ones_col, start=(c == 0), stop=(c == 15))
                    ctx.mm.inc(te.matmul(ctx.psum[bq][:, 0:1], sq, ctx.ones_col,
                                         start=(c == 0), stop=(c == 15)))
                sq_free[c % 2] = ctx.mm.now()
            mm_v = ctx.mm.now()

            @blk.vector
            def _(v, j=j, bs=bs, bq=bq):
                wt(v, mm_v)
                if kind == "ln":
                    v.tensor_copy(S16[:, j:j + 1], ctx.psum[bs][:, 0:1])
                ctx.ev.inc(v.tensor_copy(Q16[:, j:j + 1], ctx.psum[bq][:, 0:1]))
            ctx.bankw[bs] = ctx.ev.now()
            ctx.bankw[bq] = ctx.ev.now()

        @blk.vector
        def _(v):
            wt(v, ctx.ev.now())
            DR = v.drain
            S8, Q8 = S16[:, 0:8], Q16[:, 0:8]
            NM8, VP8, SD8 = NM16[:, 0:8], VP16[:, 0:8], SD16[:, 0:8]
            if kind == "ln":
                v.tensor_scalar_mul(NM8, S8, -1.0 / 2048.0)
                v.tensor_scalar_mul(VP8, Q8, 1.0 / 2048.0)
                DR()
                v.tensor_mul(SD8, NM8, NM8)
                DR()
                v.tensor_sub(VP8, VP8, SD8)
                DR()
                ctx.ev.inc(v.tensor_scalar_add(VP8, VP8, eps))
            else:
                ctx.ev.inc(v.tensor_scalar(VP8, Q8, 1.0 / 2048.0, eps,
                                           ALU.mult, ALU.add))
        evv = ctx.ev.now()

        @blk.scalar
        def _(s):
            wt(s, evv)
            ctx.sc.inc(s.activation(SD16[:, 0:8], VP16[:, 0:8], AF.Sqrt))
        scv = ctx.sc.now()

        @blk.vector
        def _(v):
            wt(v, scv)
            v.reciprocal(RS16[:, 0:8], SD16[:, 0:8])
            v.drain()
            ctx.ev.inc(v.tensor_copy(RS16[:, 0:8], RS16[:, 0:8]))

        NMb = ctx.rS[:, 0:1024]
        RSb = ctx.rS[:, 1024:2048]
        if kind == "ln":
            bcast_cols(ctx, [(NM16[:, 0:8], NMb), (RS16[:, 0:8], RSb)], 8)
        else:
            bcast_cols(ctx, [(RS16[:, 0:8], RSb)], 8)
        bc_v = ctx.ev.now()

        for c in range(16):
            esl = er.take()
            et = esl.ap

            @blk.vector
            def _(v, c=c, et=et, esl=esl):
                wt(v, bc_v, esl.reuse)
                xc = slab[:, c * TB:(c + 1) * TB]
                if kind == "ln":
                    v.tensor_add(et, xc, NMb)
                    v.drain()
                    ctx.ev.inc(v.tensor_mul(et, et, RSb))
                else:
                    wt(v, ctx.gamma_ld)
                    v.tensor_mul(et, xc, RSb)
                    v.drain()
                    ctx.ev.inc(v.tensor_single_scalar(
                        et, et, gamma[:, c:c + 1], ALU.mult))
            ev_v = ctx.ev.now()

            @blk.scalar
            def _(s, c=c, et=et, ev_v=ev_v, tb=tb, esl=esl):
                wt(s, ev_v)
                ctx.store(s, out[c * 128:(c + 1) * 128, tb * TB:(tb + 1) * TB],
                          et, esl.ssem)
            esl.release(esl.ssem.now())
    ctx.barrier()


def resadd(ctx, out, a, b, half=0.5):
    blk = ctx.blk
    ar = DRing(ctx, ctx.rE, 2, 2048, loads="la", stores="se")
    br = DRing(ctx, ctx.rF, 2, 2048, loads="lb")
    for c in range(16):
        asl = ar.take()
        bsl = br.take()

        @blk.sync
        def _(q, c=c, asl=asl, bsl=bsl):
            wt(q, asl.reuse, bsl.reuse)
            ctx.load(q, asl.ap, a[c * 128:(c + 1) * 128, :], asl.lsem)
            ctx.load(q, bsl.ap, b[c * 128:(c + 1) * 128, :], bsl.lsem)
        la, lb = asl.lsem.now(), bsl.lsem.now()

        @blk.vector
        def _(v, asl=asl, bsl=bsl):
            wt(v, la, lb)
            ctx.ev.inc(v.scalar_tensor_tensor(asl.ap, bsl.ap, half, asl.ap,
                                              ALU.mult, ALU.add))
        evv = ctx.ev.now()

        @blk.scalar
        def _(s, c=c, asl=asl, evv=evv):
            wt(s, evv)
            ctx.store(s, out[c * 128:(c + 1) * 128, :], asl.ap, asl.ssem)
        stv = asl.ssem.now()
        asl.release(stv)
        bsl.release(stv)
    ctx.barrier()


def rope(ctx, buf, nchunks, cos_hbm, sin_hbm, pswap_hbm, row0=0):
    blk = ctx.blk
    C = ctx.rC[:, 0:2048]
    S = ctx.rC[:, 2048:4096]
    lc = ctx.dsem("lc")

    @blk.sync
    def _(q):
        ctx.load(q, C, cos_hbm, lc)
        ctx.load(q, S, sin_hbm, lc)
        ctx.load(q, ctx.pswap, pswap_hbm, lc)
    cld = lc.now()

    xr = DRing(ctx, ctx.rE, 2, 2048, loads="la")
    orr = DRing(ctx, ctx.rF, 2, 2048, stores="se")
    for c in range(nchunks):
        xsl = xr.take()
        osl = orr.take()
        rows = buf[row0 + c * 128: row0 + (c + 1) * 128, :]

        @blk.sync
        def _(q, rows=rows, xsl=xsl, osl=osl):
            wt(q, cld, xsl.reuse, osl.reuse)
            ctx.load(q, xsl.ap, rows, xsl.lsem)
        ldv = xsl.lsem.now()

        for t in range(4):
            bank = t % 2
            sl = slice(t * 512, (t + 1) * 512)

            @blk.tensor
            def _(te, xsl=xsl, sl=sl, bank=bank, ldv=ldv):
                wt(te, ldv, cld)
                ctx.bankwait(te, bank)
                ctx.mm.inc(te.matmul(ctx.psum[bank], ctx.pswap, xsl.ap[:, sl],
                                     start=True, stop=True))
            mmv = ctx.mm.now()

            @blk.vector
            def _(v, xsl=xsl, osl=osl, sl=sl, bank=bank, mmv=mmv):
                wt(v, mmv)
                v.tensor_mul(osl.ap[:, sl], xsl.ap[:, sl], C[:, sl])
                v.drain()
                v.tensor_mul(xsl.ap[:, sl], ctx.psum[bank], S[:, sl])
                v.drain()
                ctx.ev.inc(v.tensor_add(osl.ap[:, sl], osl.ap[:, sl],
                                        xsl.ap[:, sl]))
            ctx.bankw[bank] = ctx.ev.now()
        evv = ctx.ev.now()

        @blk.scalar
        def _(s, rows=rows, osl=osl, evv=evv):
            wt(s, evv)
            ctx.store(s, rows, osl.ap, osl.ssem)
        stv = osl.ssem.now()
        xsl.release(stv)
        osl.release(stv)
    ctx.barrier()


def transpose_fm(ctx, out, in_, M, N):
    blk = ctx.blk
    xr = DRing(ctx, ctx.rE, 2, 2048, loads="la")
    orr = DRing(ctx, ctx.rF, 8, 512, stores="se")
    for c in range(M // 128):
        xsl = xr.take()

        @blk.sync
        def _(q, c=c, xsl=xsl):
            wt(q, xsl.reuse)
            ctx.load(q, xsl.ap[:, 0:N], in_[c * 128:(c + 1) * 128, :], xsl.lsem)
        ldv = xsl.lsem.now()

        last = None
        for b in range(N // 128):
            bank = b % 2
            osl = orr.take()

            @blk.tensor
            def _(te, b=b, xsl=xsl, bank=bank, ldv=ldv):
                wt(te, ldv)
                ctx.bankwait(te, bank)
                ctx.mm.inc(te.transpose(ctx.psum[bank][:, 0:128],
                                        xsl.ap[:, b * 128:(b + 1) * 128],
                                        ctx.idn))
            mmv = ctx.mm.now()

            @blk.vector
            def _(v, osl=osl, bank=bank, mmv=mmv):
                wt(v, mmv, osl.reuse)
                ctx.ev.inc(v.tensor_copy(osl.ap[:, 0:128],
                                         ctx.psum[bank][:, 0:128]))
            ctx.bankw[bank] = ctx.ev.now()
            evv = ctx.ev.now()

            @blk.scalar
            def _(s, b=b, c=c, osl=osl, evv=evv):
                wt(s, evv)
                ctx.store(s, out[b * 128:(b + 1) * 128, c * 128:(c + 1) * 128],
                          osl.ap[:, 0:128], osl.ssem)
            osl.release(osl.ssem.now())
            last = osl.ssem.now()
        xsl.release(last)
    ctx.barrier()


def router(ctx, WB, gwT, LN):
    blk = ctx.blk
    TB = 1024
    slab = ctx.rA
    gw = ctx.gw_sb
    lda = ctx.dsem("lda")
    lc = ctx.dsem("lc")
    so = ctx.dsem("so1")

    @blk.sync
    def _(q):
        for c in range(16):
            ctx.load(q, gw[:, c * 8:(c + 1) * 8], gwT[c * 128:(c + 1) * 128, :], lc)
    gw_ld = lc.now()

    g = ctx.gsmall
    LT = g[:, 0:64]
    ET = g[:, 64:128]
    M8 = g[:, 128:192]
    W1 = g[:, 192:256]
    M2 = g[:, 256:320]
    GS = g[:, 320:352]
    GS2 = g[:, 352:384]
    E1 = g[:, 384:416]
    E2 = g[:, 416:448]
    GSEL = g[:, 448:480]
    NM = g[:, 480:496]
    SU = g[:, 496:512]
    RV = g[:, 512:528]
    G1 = g[:, 528:544]
    T1 = g[:, 544:560]
    LG = ctx.rC[:, 0:1024]

    for tb in range(2):
        prev_stores = [(s, n) for s, n in ctx.dirty.items()]

        @blk.sync
        def _(q, tb=tb):
            wt(q, gw_ld)
            if tb > 0:
                wt(q, prev_stores)
            for c in range(16):
                ctx.load(q, slab[:, c * TB:(c + 1) * TB],
                         LN[c * 128:(c + 1) * 128, tb * TB:(tb + 1) * TB], lda)
        ld_v = lda.now()

        for t in range(2):
            bank = t % 2

            @blk.tensor
            def _(te, t=t, bank=bank):
                wt(te, ld_v, gw_ld)
                ctx.bankwait(te, bank)
                ins = None
                for c in range(16):
                    ins = te.matmul(ctx.psum[bank][0:8, :],
                                    gw[:, c * 8:(c + 1) * 8],
                                    slab[:, c * TB + t * 512: c * TB + t * 512 + 512],
                                    start=(c == 0), stop=(c == 15))
                ctx.mm.inc(ins)
            mmv = ctx.mm.now()

            @blk.vector
            def _(v, t=t, bank=bank, mmv=mmv):
                wt(v, mmv)
                ctx.ev.inc(v.tensor_copy(LG[0:8, t * 512:(t + 1) * 512],
                                         ctx.psum[bank][0:8, :]))
            ctx.bankw[bank] = ctx.ev.now()
        lg_v = ctx.ev.now()

        for j in range(8):
            LTj = LT[:, j * 8:(j + 1) * 8]
            ETj = ET[:, j * 8:(j + 1) * 8]
            M8j = M8[:, j * 8:(j + 1) * 8]
            W1j = W1[:, j * 8:(j + 1) * 8]
            M2j = M2[:, j * 8:(j + 1) * 8]
            GSj = GS[:, j * 4:(j + 1) * 4]
            GS2j = GS2[:, j * 4:(j + 1) * 4]
            E1j = E1[:, j * 4:(j + 1) * 4]
            E2j = E2[:, j * 4:(j + 1) * 4]
            GSELj = GSEL[:, j * 4:(j + 1) * 4]
            NMj = NM[:, j:j + 1]
            SUj = SU[:, j:j + 1]
            RVj = RV[:, j:j + 1]
            G1j = G1[:, j:j + 1]
            T1j = T1[:, j:j + 1]
            bank = 2 + j % 2

            @blk.tensor
            def _(te, j=j, bank=bank, lg_v=lg_v):
                wt(te, lg_v)
                ctx.bankwait(te, bank)
                ctx.mm.inc(te.transpose(ctx.psum[bank][:, 0:8],
                                        LG[0:8, j * 128:(j + 1) * 128],
                                        ctx.idn[0:8, 0:8]))
            mmv = ctx.mm.now()

            @blk.vector
            def _(v, LTj=LTj, NMj=NMj, bank=bank, mmv=mmv):
                wt(v, mmv)
                v.tensor_copy(LTj, ctx.psum[bank][:, 0:8])
                v.drain()
                v.tensor_reduce(NMj, LTj, AX.X, ALU.max, negate=True)
                ctx.ev.inc(v.drain())
            ctx.bankw[bank] = ctx.ev.now()
            nm_v = ctx.ev.now()

            @blk.scalar
            def _(s, LTj=LTj, ETj=ETj, NMj=NMj, SUj=SUj, nm_v=nm_v):
                wt(s, nm_v)
                ctx.sc.inc(s.activation(ETj, LTj, AF.Exp, bias=NMj,
                                        accum_out=SUj))
            ex_v = ctx.sc.now()

            @blk.vector
            def _(v, j=j, ETj=ETj, M8j=M8j, W1j=W1j, M2j=M2j, GSj=GSj,
                  GS2j=GS2j, E1j=E1j, E2j=E2j, GSELj=GSELj, SUj=SUj,
                  RVj=RVj, G1j=G1j, T1j=T1j, ex_v=ex_v):
                DR = v.drain
                wt(v, ex_v)
                v.reciprocal(RVj, SUj)
                DR()
                v.tensor_single_scalar(ETj, ETj, RVj, ALU.mult)
                DR()
                v.tensor_tensor(GSj, ETj[:, 0:8:2], ETj[:, 1:8:2], ALU.max)
                DR()
                v.tensor_reduce(G1j, GSj, AX.X, ALU.max)
                DR()
                v.tensor_single_scalar(E1j, GSj, G1j, ALU.is_ge)
                DR()
                v.tensor_mul(E1j, E1j, GSj)
                DR()
                v.tensor_sub(GS2j, GSj, E1j)
                DR()
                v.tensor_reduce(G1j, GS2j, AX.X, ALU.max)
                DR()
                v.tensor_single_scalar(E2j, GS2j, G1j, ALU.is_ge)
                v.tensor_single_scalar(GSELj, E1j, 0.0, ALU.is_gt)
                DR()
                v.tensor_add(GSELj, GSELj, E2j)
                DR()
                v.tensor_mul(M8j[:, 0:8:2], ETj[:, 0:8:2], GSELj)
                v.tensor_mul(M8j[:, 1:8:2], ETj[:, 1:8:2], GSELj)
                DR()
                v.tensor_reduce(T1j, M8j, AX.X, ALU.max)
                DR()
                v.tensor_single_scalar(W1j, M8j, T1j, ALU.is_ge)
                DR()
                v.tensor_mul(W1j, W1j, M8j)
                DR()
                v.tensor_sub(M2j, M8j, W1j)
                DR()
                v.tensor_reduce(T1j, M2j, AX.X, ALU.max)
                DR()
                v.tensor_single_scalar(ETj, M2j, T1j, ALU.is_ge)
                DR()
                v.tensor_mul(M2j, M2j, ETj)
                DR()
                v.tensor_add(W1j, W1j, M2j)
                ctx.ev.inc(DR())

        prev_wb = None
        for n in range(NE):
            src = W1[:, n:64:8]
            dst = ctx.rF[:, 0:1024]
            bcast_cols(ctx, [(src, dst)], 8, extra_wait=prev_wb)
            bc_v = ctx.ev.now()

            @blk.scalar
            def _(s, n=n, tb=tb, dst=dst, bc_v=bc_v):
                wt(s, bc_v)
                ctx.store(s, WB[n, :, tb * TB:(tb + 1) * TB], dst, so)
            prev_wb = so.now()
    ctx.barrier()


def moe_hidden(ctx, HDN, GU, FF, wb_hbm=None):
    blk = ctx.blk
    lc = ctx.dsem("lc")
    wb_ld = None
    if wb_hbm is not None:
        @blk.sync
        def _(q):
            ctx.load(q, ctx.rS[:, 0:2048], wb_hbm, lc)
        wb_ld = lc.now()

    gr = DRing(ctx, ctx.rE, 2, 2048, loads="la")
    ur = DRing(ctx, ctx.rF, 2, 2048, loads="lb", stores="se")
    for fc in range(FF // 128):
        gsl = gr.take()
        usl = ur.take()

        @blk.sync
        def _(q, fc=fc, gsl=gsl, usl=usl):
            wt(q, gsl.reuse, usl.reuse)
            ctx.load(q, gsl.ap, GU[fc * 128:(fc + 1) * 128, :], gsl.lsem)
            ctx.load(q, usl.ap, GU[FF + fc * 128: FF + (fc + 1) * 128, :],
                     usl.lsem)
        lg, lu = gsl.lsem.now(), usl.lsem.now()

        @blk.scalar
        def _(s, gsl=gsl):
            wt(s, lg)
            ctx.sc.inc(s.activation(gsl.ap, gsl.ap, AF.Silu))
        scv = ctx.sc.now()

        @blk.vector
        def _(v, gsl=gsl, usl=usl):
            wt(v, scv, lu, wb_ld)
            if wb_hbm is not None:
                v.tensor_mul(usl.ap, gsl.ap, usl.ap)
                v.drain()
                ctx.ev.inc(v.tensor_mul(usl.ap, usl.ap, ctx.rS[:, 0:2048]))
            else:
                ctx.ev.inc(v.tensor_mul(usl.ap, gsl.ap, usl.ap))
        evv = ctx.ev.now()

        @blk.scalar
        def _(s, fc=fc, usl=usl, evv=evv):
            wt(s, evv)
            ctx.store(s, HDN[fc * 128:(fc + 1) * 128, :], usl.ap, usl.ssem)
        stv = usl.ssem.now()
        gsl.release(stv)
        usl.release(stv)
    ctx.barrier()


def attention(ctx, YA, QKV, VT, masks_hbm, *, nh, nkv, hd, causal):
    blk = ctx.blk
    A = ctx.rA
    kh = A[:, 0:2048]
    qh = A[:, 2048:4096]
    vt = A[:, 4096:6144]
    yb = A[:, 6144:8192]
    S4 = [A[:, 8192 + i * 2048: 8192 + (i + 1) * 2048] for i in range(4)]

    def PTc(c):
        r = ctx.rE if c < 8 else ctx.rF
        return r[:, (c % 8) * 512:(c % 8) * 512 + 512]

    g = ctx.gsmall
    nm = [g[:, i:i + 1] for i in range(4)]
    su = [g[:, 4 + i:5 + i] for i in range(4)]
    rv = [g[:, 8 + i:9 + i] for i in range(4)]
    MSK = ctx.rC[:, 0:2048]
    lc = ctx.dsem("lc")
    lk = ctx.dsem("lw0")
    lq = ctx.dsem("lw1")
    so = ctx.dsem("so1")

    if causal:
        @blk.sync
        def _(q):
            for a in range(4):
                ctx.load(q, MSK[:, a * 512:(a + 1) * 512], masks_hbm[a], lc)
    msk_ld = lc.now()
    rep = nh // nkv
    y_store = None

    for kv in range(nkv):
        @blk.sync
        def _(q, kv=kv):
            wt(q, ctx.mm.now())
            ctx.load(q, kh[0:hd, :],
                     QKV[2048 + kv * hd: 2048 + (kv + 1) * hd, :], lk)
            for c in range(16):
                ctx.load(q, vt[:, c * hd:(c + 1) * hd],
                         VT[c * 128:(c + 1) * 128, kv * hd:(kv + 1) * hd], lk)
        kv_ld = lk.now()

        for qi in range(rep):
            h = kv * rep + qi

            @blk.sync
            def _(q, h=h):
                wt(q, ctx.mm.now())
                ctx.load(q, qh[0:hd, :], QKV[h * hd:(h + 1) * hd, :], lq)
            q_ld = lq.now()

            for tqg in range(4):
                EXT = (tqg + 1) * 512 if causal else 2048
                NKC = EXT // 128
                exps = []
                for i in range(4):
                    tqt = tqg * 4 + i
                    S = S4[i]
                    last_ev = last_sc = None
                    for tkb in range(EXT // 512):
                        bank = (i * (EXT // 512) + tkb) % 2

                        @blk.tensor
                        def _(te, tqt=tqt, tkb=tkb, bank=bank):
                            wt(te, kv_ld, q_ld, msk_ld)
                            ctx.bankwait(te, bank)
                            ctx.mm.inc(te.matmul(
                                ctx.psum[bank],
                                qh[0:hd, tqt * 128:(tqt + 1) * 128],
                                kh[0:hd, tkb * 512:(tkb + 1) * 512],
                                start=True, stop=True))
                        mmv = ctx.mm.now()

                        diag = causal and (tkb == tqg)
                        if diag:
                            @blk.vector
                            def _(v, i=i, tqt=tqt, tkb=tkb, bank=bank, mmv=mmv, S=S):
                                wt(v, mmv)
                                ctx.ev.inc(v.tensor_add(
                                    S[:, tkb * 512:(tkb + 1) * 512],
                                    ctx.psum[bank],
                                    MSK[:, (tqt % 4) * 512:(tqt % 4 + 1) * 512]))
                            ctx.bankw[bank] = ctx.ev.now()
                            last_ev = ctx.ev.now()
                        else:
                            @blk.vector
                            def _(v, tkb=tkb, bank=bank, mmv=mmv, S=S):
                                wt(v, mmv)
                                ctx.ev.inc(v.tensor_copy(
                                    S[:, tkb * 512:(tkb + 1) * 512],
                                    ctx.psum[bank]))
                            ctx.bankw[bank] = ctx.ev.now()
                            last_sc = ctx.ev.now()

                    @blk.vector
                    def _(v, i=i, S=S, EXT=EXT, last_ev=last_ev, last_sc=last_sc):
                        wt(v, last_ev, last_sc)
                        ctx.ev.inc(v.tensor_reduce(nm[i], S[:, 0:EXT], AX.X,
                                                   ALU.max, negate=True))
                    exps.append(ctx.ev.now())

                ex_vs = []
                for i in range(4):
                    @blk.scalar
                    def _(s, i=i, EXT=EXT):
                        wt(s, (ctx.ev, exps[i][1]))
                        S = S4[i]
                        ctx.sc.inc(s.activation(S[:, 0:EXT], S[:, 0:EXT],
                                                AF.Exp, bias=nm[i],
                                                accum_out=su[i]))
                    ex_vs.append(ctx.sc.now())

                @blk.vector
                def _(v, EXT=EXT, ex_vs=ex_vs):
                    wt(v, ex_vs[3])
                    for i in range(4):
                        v.reciprocal(rv[i], su[i])
                    v.drain()
                    for i in range(4):
                        v.tensor_single_scalar(S4[i][:, 0:EXT], S4[i][:, 0:EXT],
                                               rv[i], ALU.mult)
                    ctx.ev.inc(v.drain())
                exps = [ctx.ev.now()] * 4

                pt_evs = None
                for c in range(NKC):
                    bank = 2 + c % 2

                    @blk.tensor
                    def _(te, c=c, bank=bank, exps=exps):
                        wt(te, exps[3])
                        ctx.bankwait(te, bank)
                        ins = None
                        for i in range(4):
                            ins = te.transpose(
                                ctx.psum[bank][:, i * 128:(i + 1) * 128],
                                S4[i][:, c * 128:(c + 1) * 128], ctx.idn)
                        ctx.mm.inc(ins)
                    mv = ctx.mm.now()

                    @blk.vector
                    def _(v, c=c, bank=bank, mv=mv):
                        wt(v, mv)
                        ctx.ev.inc(v.tensor_copy(PTc(c), ctx.psum[bank]))
                    ctx.bankw[bank] = ctx.ev.now()
                    pt_evs = ctx.ev.now()

                @blk.tensor
                def _(te, EXT=EXT, NKC=NKC, pt_evs=pt_evs):
                    wt(te, pt_evs)
                    ctx.bankwait(te, 4)
                    ins = None
                    for c in range(NKC):
                        ins = te.matmul(ctx.psum[4][0:hd, :],
                                        vt[:, c * hd:(c + 1) * hd], PTc(c),
                                        start=(c == 0), stop=(c == NKC - 1))
                    ctx.mm.inc(ins)
                y_mm = ctx.mm.now()

                @blk.vector
                def _(v, tqg=tqg, y_mm=y_mm, y_store=y_store):
                    wt(v, y_mm)
                    if tqg == 0:
                        wt(v, y_store)
                    ctx.ev.inc(v.tensor_copy(yb[0:hd, tqg * 512:(tqg + 1) * 512],
                                             ctx.psum[4][0:hd, :]))
                ctx.bankw[4] = ctx.ev.now()

            y_done = ctx.ev.now()

            @blk.scalar
            def _(s, h=h, y_done=y_done):
                wt(s, y_done)
                ctx.store(s, YA[h * hd:(h + 1) * hd, :], yb[0:hd, :], so)
            y_store = so.now()
    ctx.barrier()


def weight_quant(ctx, WQ, wT, K, M, INVW):
    blk = ctx.blk
    ntk = K // 128
    ntm = M // 2048
    NTT = ntk * ntm
    acc = ctx.rC[:, 0:NTT]
    xr = DRing(ctx, ctx.rE, 2, 2048, loads="la")
    so = ctx.dsem("so2")

    for i in range(NTT):
        ck, cm = i // ntm, i % ntm
        xsl = xr.take()

        @blk.sync
        def _(q, ck=ck, cm=cm, xsl=xsl):
            wt(q, xsl.reuse)
            ctx.load(q, xsl.ap, wT[ck * 128:(ck + 1) * 128,
                                   cm * 2048:(cm + 1) * 2048], xsl.lsem)
        ldv = xsl.lsem.now()

        @blk.vector
        def _(v, i=i, xsl=xsl, ldv=ldv):
            wt(v, ldv)
            ctx.ev.inc(v.tensor_reduce(acc[:, i:i + 1], xsl.ap, AX.X, ALU.add,
                                       apply_absolute_value=True))
        xsl.release(ctx.ev.now())

    SW = ctx.gsmall[:, 0:1]
    IW = ctx.gsmall[:, 1:2]
    t11 = ctx.gsmall[:, 2:3]

    @blk.vector
    def _(v):
        wt(v, ctx.ev.now())
        v.tensor_reduce(t11, acc[:, 0:NTT], AX.X, ALU.add)
        ctx.ev.inc(v.drain())
    rs_v = ctx.ev.now()

    @blk.tensor
    def _(te, rs_v=rs_v):
        wt(te, rs_v)
        ctx.bankwait(te, 6)
        ctx.mm.inc(te.transpose(ctx.psum[6][0:1, 0:128], t11, ctx.idn))
    tr_v = ctx.mm.now()

    @blk.vector
    def _(v, tr_v=tr_v):
        wt(v, tr_v)
        row = ctx.trow[0:1, 0:128]
        v.tensor_copy(row, ctx.psum[6][0:1, 0:128])
        v.drain()
        v.tensor_reduce(ctx.trow[0:1, 128:129], row, AX.X, ALU.add)
        v.drain()
        v.tensor_scalar(ctx.trow[0:1, 129:130], ctx.trow[0:1, 128:129],
                        1.0 / (K * M), 1e-5, ALU.mult, ALU.max)
        v.drain()
        v.reciprocal(ctx.trow[0:1, 130:131], ctx.trow[0:1, 129:130])
        ctx.ev.inc(v.drain())
    ctx.bankw[6] = ctx.ev.now()
    sw_v = ctx.ev.now()

    for src_off, dst in ((130, SW), (129, IW)):
        @blk.tensor
        def _(te, src_off=src_off, sw_v=sw_v):
            wt(te, sw_v)
            ctx.bankwait(te, 7)
            ctx.mm.inc(te.matmul(ctx.psum[7][:, 0:1], ctx.ones_row,
                                 ctx.trow[0:1, src_off:src_off + 1],
                                 start=True, stop=True))
        bv = ctx.mm.now()

        @blk.vector
        def _(v, dst=dst, bv=bv):
            wt(v, bv)
            ctx.ev.inc(v.tensor_copy(dst, ctx.psum[7][:, 0:1]))
        ctx.bankw[7] = ctx.ev.now()
    swb_v = ctx.ev.now()

    @blk.scalar
    def _(s, swb_v=swb_v):
        wt(s, swb_v)
        ctx.store(s, INVW, IW, so)

    br = DRing(ctx, ctx.rE.bitcast(BF16), 4, 2048, stores="se")
    xq = DRing(ctx, ctx.rA, 4, 2048, loads="la")
    scr = ctx.rF[:, 0:2048]
    scr2 = ctx.rF[:, 2048:4096]
    scrB = ctx.rA[:, 8192:10240]
    scr2B = ctx.rA[:, 10240:12288]
    i32B = ctx.rA[:, 12288:14336].bitcast(I32)
    for i in range(0, NTT, 2):
        xa, xb = xq.take(), xq.take()
        ba, bb = br.take(), br.take()

        @blk.sync
        def _(q, i=i, xa=xa, xb=xb):
            wt(q, xa.reuse, xb.reuse)
            for k, xs in ((i, xa), (i + 1, xb)):
                ck, cm = k // ntm, k % ntm
                ctx.load(q, xs.ap, wT[ck * 128:(ck + 1) * 128,
                                      cm * 2048:(cm + 1) * 2048], xs.lsem)
        ldv = [xa.lsem.now(), xb.lsem.now()]

        @blk.vector
        def _(v, xa=xa, xb=xb, ba=ba, bb=bb, ldv=ldv, swb_v=swb_v):
            wt(v, ldv, ba.reuse, bb.reuse, swb_v)
            v.tensor_single_scalar(xa.ap, xa.ap, SW, ALU.mult)
            v.tensor_single_scalar(xb.ap, xb.ap, SW, ALU.mult)
            emit_round_pair(v, xa.ap, xb.ap, scr, scr2, ctx.rR,
                            scrB, scr2B, i32B)
            v.tensor_scalar(xa.ap, xa.ap, -1.0, 1.0, ALU.max, ALU.min)
            v.tensor_scalar(xb.ap, xb.ap, -1.0, 1.0, ALU.max, ALU.min)
            v.tensor_copy(ba.ap, xa.ap)
            ctx.ev.inc(v.tensor_copy(bb.ap, xb.ap))
        evv = ctx.ev.now()

        @blk.scalar
        def _(s, i=i, ba=ba, bb=bb, evv=evv):
            wt(s, evv)
            for k, bs in ((i, ba), (i + 1, bb)):
                ck, cm = k // ntm, k % ntm
                ctx.store(s, WQ[ck * 128:(ck + 1) * 128,
                                cm * 2048:(cm + 1) * 2048], bs.ap, bs.ssem)
        stva, stvb = ba.ssem.now(), bb.ssem.now()
        xa.release(stva)
        xb.release(stvb)
        ba.release(stva)
        bb.release(stvb)
    ctx.barrier()


def actquant(ctx, XQ, src, nchunks, SCL_hbm, INVW, extra, stats_hbm=None):
    blk = ctx.blk
    am = ctx.rS[:, 2048:4096]
    xr = DRing(ctx, ctx.rE, 2, 2048, loads="la")
    lc = ctx.dsem("lc")
    so = ctx.dsem("so3")

    if stats_hbm is None:
        @blk.vector
        def _(v):
            ctx.ev.inc(v.memset(am, 0.0))
        for c in range(nchunks):
            xsl = xr.take()

            @blk.sync
            def _(q, c=c, xsl=xsl):
                wt(q, xsl.reuse)
                ctx.load(q, xsl.ap, src[c * 128:(c + 1) * 128, :], xsl.lsem)
            ldv = xsl.lsem.now()

            @blk.vector
            def _(v, xsl=xsl, ldv=ldv):
                wt(v, ldv)
                v.drain()
                v.tensor_tensor(am, am, xsl.ap, ALU.max)
                v.drain()
                ctx.ev.inc(v.scalar_tensor_tensor(am, xsl.ap, -1.0, am,
                                                  ALU.mult, ALU.max))
            xsl.release(ctx.ev.now())
    else:
        @blk.sync
        def _(q):
            ctx.load(q, am, stats_hbm, lc)
        am_ld = lc.now()

        @blk.vector
        def _(v):
            wt(v, am_ld)
            ctx.ev.inc(v.tensor_copy(am, am))
    am_v = ctx.ev.now()

    AMAX = ctx.stats[:, 96:112]
    stg = ctx.sqst[:, 0:128]
    for j in range(16):
        bank = j % 2

        @blk.tensor
        def _(te, j=j, bank=bank, am_v=am_v):
            wt(te, am_v)
            ctx.bankwait(te, bank)
            ctx.mm.inc(te.transpose(ctx.psum[bank][:, 0:128],
                                    am[:, j * 128:(j + 1) * 128], ctx.idn))
        mv = ctx.mm.now()

        @blk.vector
        def _(v, j=j, bank=bank, mv=mv):
            wt(v, mv)
            v.drain()
            v.tensor_copy(stg, ctx.psum[bank][:, 0:128])
            v.drain()
            ctx.ev.inc(v.tensor_reduce(AMAX[:, j:j + 1], stg, AX.X, ALU.max))
        ctx.bankw[bank] = ctx.ev.now()

    SC16 = ctx.stats[:, 112:128]
    IV16 = ctx.stats[:, 128:144]
    iw = ctx.gsmall[:, 8:9]
    lc2 = ctx.dsem("lc2")

    @blk.sync
    def _(q):
        ctx.load(q, iw, INVW, lc2)
    iw_ld = lc2.now()

    @blk.vector
    def _(v):
        wt(v, ctx.ev.now())
        wt(v, iw_ld)
        v.tensor_scalar_max(AMAX, AMAX, 1e-5)
        v.drain()
        v.reciprocal(SC16, AMAX)
        v.tensor_scalar_mul(IV16, AMAX, 1.0 / 127.0)
        v.drain()
        ctx.ev.inc(v.tensor_scalar_mul(SC16, SC16, 127.0))

    SCb = ctx.rS[:, 0:2048]
    IVb = am
    bcast_cols(ctx, [(SC16, SCb), (IV16, IVb)], 16)

    @blk.vector
    def _(v):
        wt(v, ctx.ev.now())
        v.tensor_single_scalar(IVb, IVb, iw, ALU.mult)
        if extra != 1.0:
            v.drain()
            v.tensor_scalar_mul(IVb, IVb, extra)
        ctx.ev.inc(v.drain())
    scl_v = ctx.ev.now()

    @blk.scalar
    def _(s, scl_v=scl_v):
        wt(s, scl_v)
        ctx.store(s, SCL_hbm, IVb, so)

    br = DRing(ctx, ctx.rE.bitcast(BF16), 4, 2048, stores="se")
    xq = DRing(ctx, ctx.rA, 4, 2048, loads="la")
    scr = ctx.rF[:, 0:2048]
    scr2 = ctx.rF[:, 2048:4096]
    scrB = ctx.rA[:, 8192:10240]
    scr2B = ctx.rA[:, 10240:12288]
    i32B = ctx.rA[:, 12288:14336].bitcast(I32)
    for c in range(0, nchunks, 2):
        xa, xb = xq.take(), xq.take()
        ba, bb = br.take(), br.take()

        @blk.sync
        def _(q, c=c, xa=xa, xb=xb):
            wt(q, xa.reuse, xb.reuse)
            ctx.load(q, xa.ap, src[c * 128:(c + 1) * 128, :], xa.lsem)
            ctx.load(q, xb.ap, src[(c + 1) * 128:(c + 2) * 128, :], xb.lsem)
        ldv = [xa.lsem.now(), xb.lsem.now()]

        @blk.vector
        def _(v, xa=xa, xb=xb, ba=ba, bb=bb, ldv=ldv, scl_v=scl_v):
            wt(v, ldv, ba.reuse, bb.reuse, scl_v)
            v.tensor_mul(xa.ap, xa.ap, SCb)
            v.tensor_mul(xb.ap, xb.ap, SCb)
            emit_round_pair(v, xa.ap, xb.ap, scr, scr2, ctx.rR,
                            scrB, scr2B, i32B)
            v.tensor_scalar(xa.ap, xa.ap, -128.0, 127.0, ALU.max, ALU.min)
            v.tensor_scalar(xb.ap, xb.ap, -128.0, 127.0, ALU.max, ALU.min)
            v.tensor_copy(ba.ap, xa.ap)
            ctx.ev.inc(v.tensor_copy(bb.ap, xb.ap))
        evv = ctx.ev.now()

        @blk.scalar
        def _(s, c=c, ba=ba, bb=bb, evv=evv):
            wt(s, evv)
            ctx.store(s, XQ[c * 128:(c + 1) * 128, :], ba.ap, ba.ssem)
            ctx.store(s, XQ[(c + 1) * 128:(c + 2) * 128, :], bb.ap, bb.ssem)
        stva, stvb = ba.ssem.now(), bb.ssem.now()
        xa.release(stva)
        xb.release(stvb)
        ba.release(stva)
        bb.release(stvb)
    ctx.barrier()


def build_program(n_stages=None, dump=()):
    _WCACHE.clear()
    nc = bass.Bass("TRN2", target_bir_lowering=False)

    def din(name, shape, dt=F32):
        return nc.dram_tensor(name, list(shape), dt, kind="ExternalInput").ap()

    def dsc(name, shape, dt=F32):
        return nc.dram_tensor(name, list(shape), dt, kind="Internal").ap()

    x_in = din("x", [D, T])
    out = nc.dram_tensor("out", [D, T], F32, kind="ExternalOutput").ap()

    W = {}
    for name, shape in [
        ("gwT", [D, NE]), ("guT", [NE, 11, 128, 4096]), ("dnT", [NE, 8, 128, 2816]),
        ("sguT", [22, 128, 4096]), ("sdT", [16, 128, 2816]),
        ("qkv0T", [12, 128, 4096]), ("o0T", [8, 128, 4096]),
        ("qkv1T", [12, 128, 4096]), ("o1T", [8, 128, 4096]),
        ("w1T", [D, FH]), ("w2T", [FH, D]), ("gammaR", [128, 16]),
        ("R0C", [128, T]), ("R0S", [128, T]), ("R1C", [128, T]), ("R1S", [128, T]),
        ("PSW0", [128, 128]), ("PSW1", [128, 128]), ("MASKS", [4, 128, 512]),
        ("IDN", [128, 128]),
    ]:
        W[name] = din(name, shape)

    XR = dsc("XR", [D, T])
    LN = dsc("LNs", [D, T])
    RT = dsc("RT", [D, T])
    GU = dsc("GU", [2 * FS, T])
    HDN = dsc("HDN", [FS, T])
    WB = dsc("WBs", [NE, 128, T])
    QKV = dsc("QKV", [3072, T])
    VT = dsc("VTs", [T, 512])
    YA = dsc("YAs", [D, T])
    RMS = dsc("RMS", [D, T])
    H = dsc("Hs", [FH, T])
    XQ1 = dsc("XQ1", [D, T], BF16)
    XQ2 = dsc("XQ2", [FH, T], BF16)
    W1Q = dsc("W1Q", [D, FH], BF16)
    W2Q = dsc("W2Q", [FH, D], BF16)
    SCLA = dsc("SCLA", [128, T])
    SCLB = dsc("SCLB", [128, T])
    AM2 = dsc("AM2", [128, T])
    INVW1 = dsc("INVW1", [128, 1])
    INVW2 = dsc("INVW2", [128, 1])

    scr = dict(XR=XR, LN=LN, RT=RT, GU=GU, HDN=HDN, WB=WB, QKV=QKV, VT=VT,
               YA=YA, RMS=RMS, H=H, XQ1=XQ1, XQ2=XQ2, W1Q=W1Q, W2Q=W2Q,
               SCLA=SCLA, SCLB=SCLB, AM2=AM2, INVW1=INVW1, INVW2=INVW2)

    import contextlib
    with contextlib.ExitStack() as stack:
        blk = None

        class _C(Ctx):
            def semalloc(self, name):
                return stack.enter_context(nc.semaphore(name))

        h_mm = stack.enter_context(nc.semaphore("mm"))
        h_ev = stack.enter_context(nc.semaphore("ev"))
        h_sc = stack.enter_context(nc.semaphore("sc"))
        blk = stack.enter_context(nc.Block())
        ctx = _C(nc, blk)
        ctx.mm, ctx.ev, ctx.sc = Sem(h_mm), Sem(h_ev), Sem(h_sc)

        A = nc.alloc_sbuf_tensor
        ctx.rA = A("rA", [128, 16384], F32).ap()
        ctx.rB = A("rB", [128, 2048], F32).ap()
        ctx.rW0 = A("rW0", [128, 4096], F32).ap()
        ctx.rW1 = A("rW1", [128, 4096], F32).ap()
        ctx.rE = A("rE", [128, 4096], F32).ap()
        ctx.rF = A("rF", [128, 4096], F32).ap()
        ctx.rS = A("rS", [128, 4096], F32).ap()
        ctx.rC = A("rC", [128, 4096], F32).ap()
        ctx.rR = A("rR", [128, 2048], I32).ap()
        ctx.idn = A("idn", [128, 128], F32).ap()
        ctx.pswap = A("pswap", [128, 128], F32).ap()
        ctx.trow = A("trow", [128, 256], F32).ap()
        ctx.stats = A("stats", [128, 256], F32).ap()
        ctx.sqst = A("sqst", [128, 256], F32).ap()
        ctx.gw_sb = A("gw_sb", [128, 128], F32).ap()
        ctx.ones_col = A("ones_col", [128, 1], F32).ap()
        ctx.ones_row = A("ones_row", [1, 128], F32).ap()
        ctx.gsmall = A("gsmall", [128, 640], F32).ap()
        ctx.gammaR = A("gammaR_sb", [128, 16], F32).ap()
        for i in range(8):
            ctx.psum.append(nc.alloc_psum_tensor(f"p{i}", [128, 512], F32).ap())

        lc = ctx.dsem("lc")

        @blk.sync
        def _(q):
            ctx.load(q, ctx.idn, W["IDN"], lc)
            ctx.load(q, ctx.gammaR, W["gammaR"], lc)
        ctx.gamma_ld = lc.now()

        @blk.vector
        def _(v):
            v.memset(ctx.ones_col, 1.0)
            ctx.ev.inc(v.memset(ctx.ones_row, 1.0))

        @blk.tensor
        def _(te):
            wt(te, ctx.gamma_ld)
            wt(te, ctx.ev.now())

        stages = []

        def stage(fn, *a, **k):
            stages.append((fn, a, k))

        def moe_block(src):
            stage(norm_fm, ctx, LN, src, kind="ln")
            stage(router, ctx, WB, W["gwT"], LN)
            for n in range(NE):
                stage(gemm, ctx, GU, W["guT"][n], LN, D, 2 * FE, T)
                stage(moe_hidden, ctx, HDN, GU, FE, wb_hbm=WB[n])
                stage(gemm, ctx, RT, W["dnT"][n], HDN, FE, D, T,
                      addf=(RT if n > 0 else None))
            stage(gemm, ctx, GU, W["sguT"], LN, D, 2 * FS, T)
            stage(moe_hidden, ctx, HDN, GU, FS)
            stage(gemm, ctx, RT, W["sdT"], HDN, FS, D, T, addf=RT, t_block=512)
            stage(resadd, ctx, XR, src, RT, 0.5)

        def attn_block(layer):
            qkvT = W["qkv0T"] if layer == 0 else W["qkv1T"]
            oT = W["o0T"] if layer == 0 else W["o1T"]
            RCc = W["R0C"] if layer == 0 else W["R1C"]
            RSs = W["R0S"] if layer == 0 else W["R1S"]
            PSW = W["PSW0"] if layer == 0 else W["PSW1"]
            sm = SM0 if layer == 0 else SM1
            nh, nkv, hd = (NH0, KV0, HD0) if layer == 0 else (NH1, KV1, HD1)
            stage(norm_fm, ctx, LN, XR, kind="ln")
            stage(gemm, ctx, QKV, qkvT, LN, D, 3072, T, scale_mtiles=(16, sm))
            stage(rope, ctx, QKV, 16, RCc, RSs, PSW, row0=0)
            stage(rope, ctx, QKV, 4, RCc, RSs, PSW, row0=2048)
            stage(transpose_fm, ctx, VT, QKV[2560:3072, :], 512, T)
            stage(attention, ctx, YA, QKV, VT, W["MASKS"],
                  nh=nh, nkv=nkv, hd=hd, causal=(layer == 0))
            stage(gemm, ctx, XR, oT, YA, D, D, T, addf=XR)

        def bitffn_block(first, last):
            stage(norm_fm, ctx, LN, XR, kind="ln")
            stage(norm_fm, ctx, RMS, LN, kind="rms", eps=1e-6, gamma=ctx.gammaR)
            if first:
                stage(weight_quant, ctx, W1Q, W["w1T"], D, FH, INVW1)
                stage(weight_quant, ctx, W2Q, W["w2T"], FH, D, INVW2)
            stage(actquant, ctx, XQ1, RMS, 16, SCLA, INVW1, 1.0)
            stage(gemm, ctx, H, W1Q, XQ1, D, FH, T, dt=BF16, mul_hbm=SCLA,
                  act=AF.Gelu, stats=ctx.rS[:, 2048:4096], stats_hbm=AM2,
                  t_block=2048)
            stage(actquant, ctx, XQ2, H, 64, SCLB, INVW2, 0.5, stats_hbm=AM2)
            stage(gemm, ctx, (out if last else XR), W2Q, XQ2, FH, D, T,
                  dt=BF16, mul_hbm=SCLB, addf=XR, t_block=512)

        moe_block(x_in)
        attn_block(0)
        moe_block(XR)
        bitffn_block(first=True, last=False)
        attn_block(1)
        bitffn_block(first=False, last=True)

        for i, (fn, a, k) in enumerate(stages):
            if n_stages is not None and i >= n_stages:
                break
            ctx.begin()
            fn(*a, **k)

        if dump:
            for name in dump:
                ap = scr[name]
                dbg = nc.dram_tensor("dbg_" + name, list(ap.shape), ap.dtype,
                                     kind="ExternalOutput").ap()

                @blk.sync
                def _(q, ap=ap, dbg=dbg):
                    wt(q, ctx.barwaits)
                    ctx.store(q, dbg, ap, ctx.dsem("so0"))

        final = list(ctx.barwaits) + [(s, n) for s, n in ctx.dirty.items()]

        @blk.sync
        def _(q):
            wt(q, final)
    return nc


def stage_inputs(inputs):
    g = {k: np.asarray(v) for k, v in inputs.items()}
    f = np.float32
    ct = np.ascontiguousarray

    def TT(a):
        return ct(np.asarray(a).T)

    m = {}
    m["gwT"] = TT(g["L0_ffn_gate_weight"])
    m["guT"] = ct(np.concatenate([
        np.transpose(g["L0_ffn_gate_w"], (0, 2, 1)),
        np.transpose(g["L0_ffn_up_w"], (0, 2, 1))], axis=2))
    m["dnT"] = ct(np.transpose(g["L0_ffn_down_w"], (0, 2, 1)))
    m["sguT"] = ct(np.concatenate([TT(g["L0_ffn_sh_gate_w"]),
                                   TT(g["L0_ffn_sh_up_w"])], axis=1))
    m["sdT"] = TT(g["L0_ffn_sh_down_w"])
    m["qkv0T"] = ct(np.concatenate([TT(g["L0_attn_wq"]), TT(g["L0_attn_wk"]),
                                    TT(g["L0_attn_wv"])], axis=1))
    m["o0T"] = TT(g["L0_attn_wo"])
    m["qkv1T"] = ct(np.concatenate([TT(g["L1_attn_Wq"]), TT(g["L1_attn_Wk"]),
                                    TT(g["L1_attn_Wv"])], axis=1))
    m["o1T"] = TT(g["L1_attn_Wo"])
    m["w1T"] = TT(g["L1_ffn_w1"])
    m["w2T"] = TT(g["L1_ffn_w2"])
    m["gammaR"] = ct(np.asarray(g["L1_ffn_gamma"], f).reshape(16, 128).T)

    def group_w(w, esz=4):
        K, M = w.shape
        KC = K // 128
        wcap = 4096 * 4 // esz
        GM = 2 if KC * 256 <= wcap else 1
        gw = GM * 128
        NG = M // gw
        blocks = []
        for gg in range(NG):
            b = w[:, gg * gw:(gg + 1) * gw].reshape(KC, 128, gw)
            blocks.append(np.transpose(b, (1, 0, 2)).reshape(128, KC * gw))
        return ct(np.stack(blocks, axis=0))

    for k in ("qkv0T", "o0T", "qkv1T", "o1T", "sguT", "sdT"):
        m[k] = group_w(m[k])
    m["guT"] = ct(np.stack([group_w(m["guT"][n]) for n in range(NE)], axis=0))
    m["dnT"] = ct(np.stack([group_w(m["dnT"][n]) for n in range(NE)], axis=0))

    t = np.arange(T, dtype=np.float64)
    d = np.arange(128)
    th0 = 1.0 / (10000.0 ** (np.arange(0, HD0, 2, dtype=np.float64) / HD0))
    fr0 = th0[:, None] * t[None, :]
    i0 = (d % 64) // 2
    sgn0 = np.where(d % 2 == 0, -1.0, 1.0)
    m["R0C"] = ct(np.cos(fr0)[i0, :].astype(f))
    m["R0S"] = ct((sgn0[:, None] * np.sin(fr0)[i0, :]).astype(f))
    th1 = 1.0 / (10000.0 ** (np.arange(0, HD1, 2, dtype=np.float64) / HD1))
    fr1 = th1[:, None] * t[None, :]
    i1 = d % 64
    sgn1 = np.where(d < 64, -1.0, 1.0)
    m["R1C"] = ct(np.cos(fr1)[i1, :].astype(f))
    m["R1S"] = ct((sgn1[:, None] * np.sin(fr1)[i1, :]).astype(f))

    p0 = np.zeros((128, 128), f)
    p0[np.arange(128), np.arange(128) ^ 1] = 1.0
    m["PSW0"] = p0
    p1 = np.zeros((128, 128), f)
    p1[np.arange(128), np.arange(128) ^ 64] = 1.0
    m["PSW1"] = p1

    masks = np.zeros((4, 128, 512), f)
    p = np.arange(128)[:, None]
    c = np.arange(512)[None, :]
    for a_ in range(4):
        masks[a_] = np.where(c <= 128 * a_ + p, 0.0, NEGBIG)
    m["MASKS"] = masks
    m["IDN"] = np.eye(128, dtype=f)
    for k in m:
        assert m[k].dtype == np.float32, (k, m[k].dtype)
    return m


def kernel(**inputs):
    m = stage_inputs(inputs)
    x = np.asarray(inputs["L0_attn_x"])
    nc = build_program(n_stages=N_STAGES, dump=tuple(DUMP))
    in_maps = []
    for b in range(8):
        d = dict(m)
        d["x"] = np.ascontiguousarray(x[b].T)
        in_maps.append(d)
    res = run_bass_kernel_spmd(nc, in_maps, core_ids=list(range(8)))
    outs = [np.asarray(r["out"]).T for r in res.results]
    return np.stack(outs, axis=0)
